# version 08
# baseline (speedup 1.0000x reference)
import numpy as np
import ml_dtypes
from contextlib import ExitStack
import concourse.bass as bass
import concourse.mybir as mybir
from concourse.bass_utils import run_bass_kernel_spmd

F32 = mybir.dt.float32
BF16 = mybir.dt.bfloat16
U32 = mybir.dt.uint32
U8 = mybir.dt.uint8
AF = mybir.ActivationFunctionType
ALU = mybir.AluOpType
AX = mybir.AxisListType

NCORE = 8
T = 4096
D = 1024
S = 2048
NT = 8
TT = 512
I_FFN = 2816
NI = 22
DFF = 4096
NEXP = 8
LCAP = 768
NH = 16
DH = 64
BIG = 1.0e6
C05, C1_, C2_ = 6291456.0, 12582912.0, 25165824.0
INV6 = 1.0 / 6.0

W_NAMES = [
    "L0_attn_Wq", "L0_attn_bq", "L0_attn_Wk", "L0_attn_bk", "L0_attn_Wv",
    "L0_attn_bv", "L0_attn_Wo", "L0_attn_bo", "L0_ffn_w_gate", "L0_ffn_w_up",
    "L0_ffn_w_down", "L1_attn_W_attn", "L1_attn_b_attn", "L1_attn_W_proj",
    "L1_attn_b_proj", "L1_ffn_Wr", "L1_ffn_W1", "L1_ffn_W2",
]
W_SHAPES = {
    "L0_attn_Wq": [D, D], "L0_attn_bq": [D], "L0_attn_Wk": [D, D],
    "L0_attn_bk": [D], "L0_attn_Wv": [D, D], "L0_attn_bv": [D],
    "L0_attn_Wo": [D, D], "L0_attn_bo": [D],
    "L0_ffn_w_gate": [I_FFN, D], "L0_ffn_w_up": [I_FFN, D],
    "L0_ffn_w_down": [D, I_FFN],
    "L1_attn_W_attn": [D, 3 * D], "L1_attn_b_attn": [3 * D],
    "L1_attn_W_proj": [D, D], "L1_attn_b_proj": [D],
    "L1_ffn_Wr": [D, NEXP], "L1_ffn_W1": [NEXP, D, DFF],
    "L1_ffn_W2": [NEXP, DFF, D],
}


class Sem:
    def __init__(self, nc, ctx, name):
        self.h = ctx.enter_context(nc.semaphore(name))
        self.n = 0

    def tick(self, instr, k=1):
        instr.then_inc(self.h, k)
        self.n += k
        return self.n

    def dtick(self, instr):
        return self.tick(instr, 16)


class B:

    def __init__(self, nc, ctx, dbg=()):
        self.nc = nc
        self.ctx = ctx
        self.dbg = set(dbg)
        self.dbg_list = []
        self.in_ap = {}
        self.scr = {}
        self.in_ap["x"] = nc.dram_tensor("x", [T, D], F32, kind="ExternalInput").ap()
        for nm in W_NAMES:
            self.in_ap[nm] = nc.dram_tensor(nm, W_SHAPES[nm], F32,
                                            kind="ExternalInput").ap()
        self.out_ap = nc.dram_tensor("out", [T, D], F32, kind="ExternalOutput").ap()
        def scrd(name, shape, dt):
            kind = "ExternalOutput" if name in self.dbg else "Internal"
            self.scr[name] = nc.dram_tensor(name, shape, dt, kind=kind).ap()
        scrd("s_q0", [D, T], F32)
        scrd("s_xq", [D, T], BF16)
        scrd("s_wgq", [D, I_FFN], BF16)
        scrd("s_wuq", [D, I_FFN], BF16)
        scrd("s_wdq", [I_FFN, D], BF16)
        scrd("s_hq", [I_FFN, T], BF16)
        scrd("s_ffn0", [D, T], F32)
        scrd("s_x1", [T, D], F32)
        scrd("s_xn1", [T, D], F32)
        scrd("s_xn1b", [T, D], BF16)
        scrd("s_qk", [2 * D, T], BF16)
        scrd("s_v", [T, NH * 65], BF16)
        scrd("s_attn1", [T, D], F32)
        scrd("s_negm", [2, 16], F32)
        scrd("s_D", [NEXP * LCAP, D], BF16)
        scrd("s_eout", [NEXP * LCAP, D], F32)
        scrd("s_rt", [T], F32)
        scrd("s_rt2", [T], F32)
        scrd("s_rt3", [T], F32)

        self.dS = Sem(nc, ctx, "dS")
        self.dG = Sem(nc, ctx, "dG")
        self.eT = Sem(nc, ctx, "eT")
        self.eV = Sem(nc, ctx, "eV")
        self.eS = Sem(nc, ctx, "eS")
        self.eG = Sem(nc, ctx, "eG")
        self.allsems = [self.dS, self.dG, self.eT, self.eV, self.eS, self.eG]
        self.bar = []

        def sb(name, shape, dt=F32):
            return ctx.enter_context(nc.sbuf_tensor(name, shape, dt))
        self.sb = sb

        self.iotaP = sb("iotaP", [128, 1])
        self.iota8 = sb("iota8", [128, 8])
        self.a9 = sb("a9", [128, 8])
        self.iota8v = sb("iota8v", [8, 1])
        self.iota768e = sb("iota768e", [8, 1])
        self.identf = sb("identf", [128, 128])
        self.identb = sb("identb", [128, 128], BF16)
        self.maskM = sb("maskM", [128, 896], BF16)
        self.h2 = sb("h2", [128, 2], BF16)
        self.ones1f = sb("ones1f", [1, 128])
        self.ones8f = sb("ones8f", [8, 1])
        self.ones1f8 = sb("ones1f8", [1, 8])
        self.onetokb = sb("onetokb", [1, 128], BF16)
        self.bq_c = sb("bq_c", [128, 8])
        self.bk_c = sb("bk_c", [128, 8])
        self.bv_c = sb("bv_c", [128, 8])
        self.bo_c = sb("bo_c", [128, 8])
        self.battn_c = sb("battn_c", [128, 24])
        self.bproj_c = sb("bproj_c", [128, 8])
        self.kvacc = sb("kvacc", [128, 16])
        self.ksacc = sb("ksacc", [128, 16])
        self.qrm = sb("qrm", [128, 16])
        self.krm = sb("krm", [128, 16])
        self.idx_tm = sb("idx_tm", [128, 32])
        self.pmax_tm = sb("pmax_tm", [128, 32])
        self.kept_tm = sb("kept_tm", [128, 32])
        self.factor_tm = sb("factor_tm", [128, 32])
        self.dest_tm = sb("dest_tm", [128, 32], U32)
        self.negm32 = sb("negm32", [128, 32])

    def barrier(self):
        self.bar = [(s, s.n) for s in self.allsems]

    def bwait(self, eng):
        for sem, val in self.bar:
            if val > 0:
                eng.wait_ge(sem.h, val)

    def gap(self, eng, sem, instr):
        n = sem.tick(instr)
        eng.wait_ge(sem.h, n)

    def dbg_store(self, name, ap, waits):
        if name in self.dbg:
            shape = list(ap.shape)
            dt = ap.dtype
            d = self.nc.dram_tensor("dbg_" + name, shape, dt,
                                    kind="ExternalOutput").ap()
            self.dbg_list.append((d, ap, list(waits)))

    def flush_dbg_and_finish(self):
        nc = self.nc
        with nc.Block() as blk:
            def g_fn(g):
                self.bwait(g)
                for d, ap, waits in self.dbg_list:
                    for sem, val in waits:
                        g.wait_ge(sem.h, val)
                    self.dG.dtick(g.dma_start(out=d, in_=ap))
                g.wait_ge(self.dS.h, self.dS.n)
                g.wait_ge(self.dG.h, self.dG.n)
            blk.gpsimd(g_fn)

    def quant_job(self, x_ap, out_ap, ncols):
        b = self

        class QJ:
            def v1(self, v):
                ng = ncols // 16
                x3 = x_ap.rearrange("p (g k) -> p g k", k=16)
                b.gap(v, b.eV, v.tensor_reduce(b.qsc[:, 0:ng].unsqueeze(2), x3,
                                               AX.X, ALU.max,
                                               apply_absolute_value=True))
                b.gap(v, b.eV, v.tensor_scalar(b.qsc[:, 0:ng], b.qsc[:, 0:ng],
                                               INV6, 1e-12, ALU.mult, ALU.max))
                b.gap(v, b.eV, v.reciprocal(b.qrs[:, 0:ng], b.qsc[:, 0:ng]))
                rb = b.qrs[:, 0:ng].unsqueeze(2).broadcast_to([128, ng, 16])
                prev_s = getattr(b, "q_prev_s", None)
                if prev_s is not None:
                    v.wait_ge(b.eS.h, prev_s)
                ins = v.tensor_tensor(b.qu[:, 0:ncols].rearrange(
                    "p (g k) -> p g k", k=16), x3, rb, ALU.mult)
                self.u_done = b.eV.tick(ins)

            def s(self, s):
                s.wait_ge(b.eV.h, self.u_done)
                u = b.qu[:, 0:ncols]
                s.activation(b.qu2[:, 0:ncols], u, AF.Square)
                s.activation(b.qq[:, 0:ncols], u, AF.Copy, bias=C05)
                s.activation(b.qr05[:, 0:ncols], b.qq[:, 0:ncols], AF.Copy,
                             bias=-C05)
                s.activation(b.qr2[:, 0:ncols], u, AF.Copy, bias=C1_)
                ins = s.activation(b.qq[:, 0:ncols], b.qr2[:, 0:ncols],
                                   AF.Copy, bias=-C1_)
                self.s_done = b.eS.tick(ins)
                b.q_prev_s = self.s_done

            def g(self, g):
                pass

            def v2(self, v):
                v.wait_ge(b.eS.h, self.s_done)
                u = b.qu[:, 0:ncols]
                v.tensor_scalar(b.qr2[:, 0:ncols], u, C2_, -C2_, ALU.add,
                                ALU.add)
                v.tensor_scalar(b.qm1[:, 0:ncols], b.qu2[:, 0:ncols],
                                4.515625, None, ALU.is_lt)
                v.tensor_scalar(b.qm3[:, 0:ncols], b.qu2[:, 0:ncols], 16.0,
                                None, ALU.is_ge)
                v.copy_predicated(b.qq[:, 0:ncols], b.qm1[:, 0:ncols],
                                  b.qr05[:, 0:ncols])
                v.copy_predicated(b.qq[:, 0:ncols], b.qm3[:, 0:ncols],
                                  b.qr2[:, 0:ncols])
                ng = ncols // 16
                scb = b.qsc[:, 0:ng].unsqueeze(2).broadcast_to([128, ng, 16])
                ins = v.tensor_tensor(
                    out_ap.rearrange("p (g k) -> p g k", k=16),
                    b.qq[:, 0:ncols].rearrange("p (g k) -> p g k", k=16),
                    scb, ALU.mult)
                self.done = b.eV.tick(ins)

        return QJ()



def build(dbg=(), upto="p1c"):
    nc = bass.Bass("TRN2", target_bir_lowering=False)
    ctx = ExitStack()
    with ctx:
        b = B(nc, ctx, dbg=dbg)
        phase_setup(b)
        order = ["p0w", "p0a", "p0c1", "p0c2", "p0b", "p1a", "p1b", "p1c"]
        fns = {"p0w": phase_p0w, "p0a": phase_p0a, "p0c1": phase_p0c1,
               "p0c2": phase_p0c2, "p0b": phase_p0b, "p1a": phase_p1a,
               "p1b": phase_p1b, "p1c": phase_p1c}
        with ExitStack() as qstk:
            QW = I_FFN
            qsb = lambda n, s, d=F32: qstk.enter_context(
                nc.sbuf_tensor(n, s, d))
            b.qsc = qsb("qsc", [128, QW // 16])
            b.qrs = qsb("qrs", [128, QW // 16])
            b.qu = qsb("qu", [128, QW])
            b.qr05 = qsb("qr05", [128, QW])
            b.qq = qsb("qq", [128, QW])
            b.qr2 = qsb("qr2", [128, QW])
            b.qm1 = qsb("qm1", [128, QW], U8)
            b.qm3 = qsb("qm3", [128, QW], U8)
            b.qu2 = qsb("qu2", [128, QW])
            b.qcb = qsb("qcb", [128, 4])
            with nc.Block() as blkq:
                def v_qc(v):
                    v.memset(b.qcb[:, 0:1], C05)
                    v.memset(b.qcb[:, 1:2], -C05)
                    v.memset(b.qcb[:, 2:3], C1_)
                    ins = v.memset(b.qcb[:, 3:4], -C1_)
                    b.eV.tick(ins)
                blkq.vector(v_qc)
            for ph in ["p0w", "p0a", "p0c1"]:
                fns[ph](b)
                b.barrier()
                if ph == upto:
                    break
        if order.index(upto) > order.index("p0c1"):
            for ph in ["p0c2", "p0b", "p1a", "p1b", "p1c"]:
                fns[ph](b)
                b.barrier()
                if ph == upto:
                    break
        b.flush_dbg_and_finish()
    return nc


def phase_setup(b):
    nc = b.nc
    with ExitStack() as stk:
        iotaF = stk.enter_context(nc.sbuf_tensor("su_iotaF", [128, 896], F32))
        brows = {}
        for nm, n in (("L0_attn_bq", 8), ("L0_attn_bk", 8), ("L0_attn_bv", 8),
                      ("L0_attn_bo", 8), ("L1_attn_b_attn", 24),
                      ("L1_attn_b_proj", 8)):
            brows[nm] = stk.enter_context(
                nc.sbuf_tensor("su_br_" + nm, [n, 128], F32))
        with nc.Block() as blk:
            def g_fn(g):
                for tile, pat, cm in ((iotaF, [[1, 896]], 0),
                                      (b.iotaP, [[1, 1]], 1),
                                      (b.iota8, [[1, 8]], 0),
                                      (b.iota8v, [[1, 1]], 1),
                                      (b.iota768e, [[1, 1]], LCAP)):
                    ins = g.iota(tile[:], pat, channel_multiplier=cm,
                                 allow_small_or_imprecise_dtypes=True)
                b.eG.tick(ins)

            def sy_fn(sy):
                for nm, tile in brows.items():
                    n = tile.shape[0]
                    ap = b.in_ap[nm].rearrange("(m p) -> m p", p=128)
                    b.dS.dtick(sy.dma_start(out=tile[:], in_=ap))

            def v_fn(v):
                v.wait_ge(b.eG.h, b.eG.n)
                for tile, val in ((b.ones1f, 1.0), (b.ones8f, 1.0),
                                  (b.ones1f8, 1.0), (b.onetokb, 1.0),
                                  (b.kvacc, 0.0), (b.ksacc, 0.0),
                                  (b.qrm, 0.0), (b.krm, 0.0)):
                    v.memset(tile[:], val)
                v.tensor_scalar(b.identf[:], iotaF[:, 0:128], b.iotaP[:, 0:1],
                                None, ALU.is_equal)
                v.tensor_scalar(b.identb[:], iotaF[:, 0:128], b.iotaP[:, 0:1],
                                None, ALU.is_equal)
                v.tensor_scalar(b.maskM[:], iotaF[:], b.iotaP[:, 0:1], 384.0,
                                ALU.subtract, ALU.is_ge)
                v.tensor_scalar(b.a9[:], b.iota8[:], -1.0, 9.0, ALU.mult,
                                ALU.add)
                v.tensor_scalar(b.h2[:, 0:1], b.iotaP[:], 64.0, None, ALU.is_lt)
                ins = v.tensor_scalar(b.h2[:, 1:2], b.iotaP[:], 64.0, None,
                                      ALU.is_ge)
                b.eV.tick(ins)

            blk.gpsimd(g_fn)
            blk.sync(sy_fn)
            blk.vector(v_fn)

        with nc.Block() as blk2, ExitStack() as st:
            psb = st.enter_context(nc.psum_tensor("ps_bias", [128, 128], F32))
            items = [(brows["L0_attn_bq"], b.bq_c, 8),
                     (brows["L0_attn_bk"], b.bk_c, 8),
                     (brows["L0_attn_bv"], b.bv_c, 8),
                     (brows["L0_attn_bo"], b.bo_c, 8),
                     (brows["L1_attn_b_attn"], b.battn_c, 24),
                     (brows["L1_attn_b_proj"], b.bproj_c, 8)]
            recs = {}
            for i, (srct, dst, n) in enumerate(items):
                def t_one(t, i=i, srct=srct, n=n):
                    if i == 0:
                        t.wait_ge(b.dS.h, b.dS.n)
                        t.wait_ge(b.eV.h, b.eV.n)
                    else:
                        t.wait_ge(b.eV.h, recs[i - 1])
                    recs[("t", i)] = b.eT.tick(
                        t.transpose(psb[0:128, 0:n], srct[0:n, :],
                                    b.identf[0:n, 0:n]))

                def v_one(v, i=i, dst=dst, n=n):
                    v.wait_ge(b.eT.h, recs[("t", i)])
                    recs[i] = b.eV.tick(
                        v.tensor_copy(dst[:, 0:n], psb[0:128, 0:n]))

                blk2.tensor(t_one)
                blk2.vector(v_one)
    b.barrier()


def phase_p0w(b):
    nc = b.nc
    with ExitStack() as st:
        wch = st.enter_context(nc.sbuf_tensor("p0w_wch", [128, I_FFN], F32))
        wqt = st.enter_context(nc.sbuf_tensor("p0w_wqt", [128, I_FFN], BF16))
        bands = [st.enter_context(nc.sbuf_tensor(f"p0w_bd{k}", [128, I_FFN], BF16))
                 for k in range(8)]
        pst = [st.enter_context(nc.psum_tensor(f"p0w_ps{j}", [128, 512], BF16))
               for j in range(2)]

        def do_weight(wname, outname, nrowch, rowlen, nkc):
            w_in = b.in_ap[wname]
            w_out = b.scr[outname]
            batch = max(1, I_FFN // rowlen)
            kgroups = [(k, min(k + 8, nkc)) for k in range(0, nkc, 8)]
            for (k0, k1) in kgroups:
                last_copy = {}
                c = 0
                while c < nrowch:
                    nload = min(batch, nrowch - c)
                    ncols = nload * rowlen
                    with nc.Block() as blk:
                        rec = {}
                        qj = b.quant_job(wch[:, 0:ncols], wqt[:, 0:ncols], ncols)

                        def sy_all(sy, c=c, nload=nload):
                            b.bwait(sy)
                            sy.wait_ge(b.eV.h, b.eV.n)
                            sy.wait_ge(b.eT.h, b.eT.n)
                            for j in range(nload):
                                b.dS.dtick(sy.dma_start(
                                    out=wch[:, j * rowlen:(j + 1) * rowlen],
                                    in_=w_in[128 * (c + j):128 * (c + j + 1), :]))
                            rec["ld"] = b.dS.n

                        blk.sync(sy_all)
                        blk.vector(lambda v: (v.wait_ge(b.dS.h, rec["ld"]),
                                              qj.v1(v)))
                        blk.scalar(qj.s)
                        blk.gpsimd(qj.g)
                        blk.vector(qj.v2)

                        items = [(j, kk) for j in range(nload)
                                 for kk in range(k0, k1)]
                        for gi in range(0, len(items), 4):
                            grp = items[gi:gi + 4]
                            par = (gi // 4) % 2

                            def t_grp(t, grp=grp, par=par):
                                t.wait_ge(b.eV.h, qj.done)
                                prev = last_copy.get(par)
                                if prev:
                                    t.wait_ge(b.eV.h, prev)
                                for qi, (j, kk) in enumerate(grp):
                                    col0 = j * rowlen + 128 * kk
                                    ins = t.transpose(
                                        pst[par][:, 128 * qi:128 * (qi + 1)],
                                        wqt[:, col0:col0 + 128], b.identb[:])
                                rec[("t", gi)] = b.eT.tick(ins)

                            def v_grp(v, grp=grp, par=par, gi=gi, c=c):
                                v.wait_ge(b.eT.h, rec[("t", gi)])
                                for qi, (j, kk) in enumerate(grp):
                                    cc = c + j
                                    ins = v.tensor_copy(
                                        bands[kk - k0][:, 128 * cc:128 * (cc + 1)],
                                        pst[par][:, 128 * qi:128 * (qi + 1)])
                                last_copy[par] = b.eV.tick(ins)

                            blk.tensor(t_grp)
                            blk.vector(v_grp)
                    c += nload
                with nc.Block() as blk:
                    def sy_st(sy, k0=k0, k1=k1):
                        sy.wait_ge(b.eV.h, b.eV.n)
                        for kk in range(k0, k1):
                            b.dS.dtick(sy.dma_start(
                                out=w_out[128 * kk:128 * (kk + 1), 0:128 * nrowch],
                                in_=bands[kk - k0][:, 0:128 * nrowch]))
                    blk.sync(sy_st)
                b.barrier()

        do_weight("L0_ffn_w_gate", "s_wgq", NI, D, 8)
        do_weight("L0_ffn_w_up", "s_wuq", NI, D, 8)
        do_weight("L0_ffn_w_down", "s_wdq", 8, I_FFN, NI)
    b.barrier()


def phase_p0a(b):
    nc = b.nc
    with ExitStack() as st:
        sbn = lambda n, s, d=F32: st.enter_context(nc.sbuf_tensor(n, s, d))
        xt = sbn("p0a_xt", [128, 4096])
        xnt = sbn("p0a_xnt", [128, 4096])
        sqs = xnt
        xnb = sbn("p0a_xnb", [128, 4096], BF16)
        xqt = sbn("p0a_xqt", [128, 4096], BF16)
        xnfm = [sbn(f"p0a_xnfm{k}", [128, 512], BF16) for k in range(8)]
        xqfm = [sbn(f"p0a_xqfm{k}", [128, 512], BF16) for k in range(8)]
        Wq_sb = [sbn(f"p0a_wq{k}", [128, 1024], BF16) for k in range(8)]
        Wk_sb = [sbn(f"p0a_wk{k}", [128, 1024], BF16) for k in range(8)]
        Wv_sb = [sbn(f"p0a_wv{k}", [128, 1024], BF16) for k in range(8)]
        qsto = [sbn(f"p0a_qs{k}", [128, 512]) for k in range(8)]
        lsum = sbn("p0a_lsum", [128, 4])
        lssq = sbn("p0a_lssq", [128, 4])
        ltmp = sbn("p0a_ltmp", [128, 4])
        lvar = sbn("p0a_lvar", [128, 4])
        lsrt = sbn("p0a_lsrt", [128, 4])
        lrstd = sbn("p0a_lrstd", [128, 4])
        lmean = sbn("p0a_lmean", [128, 4])
        lred = sbn("p0a_lred", [128, 1])
        fa = sbn("p0a_fa", [128, 512])
        fe = sbn("p0a_fe", [128, 512])
        fr = sbn("p0a_fr", [128, 512])
        kf = sbn("p0a_kf", [128, 512])
        vp = sbn("p0a_vp", [128, 512])
        zz = sbn("p0a_zz", [128, 512])
        pst = [st.enter_context(nc.psum_tensor(f"p0a_pst{j}", [128, 512], BF16))
               for j in range(2)]
        pq_ = [st.enter_context(nc.psum_tensor(f"p0a_pq{j}", [128, 512], F32))
               for j in range(2)]
        pk_ = [st.enter_context(nc.psum_tensor(f"p0a_pk{j}", [128, 512], F32))
               for j in range(2)]
        pv_ = [st.enter_context(nc.psum_tensor(f"p0a_pv{j}", [128, 512], F32))
               for j in range(2)]

        with nc.Block() as blk:
            def g_w(g):
                b.bwait(g)
                for wnm, tiles in (("L0_attn_Wq", Wq_sb), ("L0_attn_Wk", Wk_sb),
                                   ("L0_attn_Wv", Wv_sb)):
                    for k in range(8):
                        b.dG.dtick(g.dma_start(
                            out=tiles[k][:],
                            in_=b.in_ap[wnm][128 * k:128 * (k + 1), :]))
            blk.gpsimd(g_w)
        w_ld = b.dG.n

        for t_i in range(NT):
            bb = t_i // 4
            rec = {}
            with nc.Block() as blk:
                qjs = [b.quant_job(xnt[:, 2048 * hh:2048 * (hh + 1)],
                                   xqt[:, 2048 * hh:2048 * (hh + 1)], 2048)
                       for hh in range(2)]

                def sy_ld(sy, t_i=t_i):
                    b.bwait(sy)
                    sy.wait_ge(b.eV.h, b.eV.n)
                    sy.wait_ge(b.eS.h, b.eS.n)
                    sy.wait_ge(b.eT.h, b.eT.n)
                    sy.wait_ge(b.eG.h, b.eG.n)
                    for ch in range(4):
                        r0 = TT * t_i + 128 * ch
                        b.dS.dtick(sy.dma_start(
                            out=xt[:, 1024 * ch:1024 * (ch + 1)],
                            in_=b.in_ap["x"][r0:r0 + 128, :]))
                    rec["ld"] = b.dS.n

                def s_sq(s):
                    s.wait_ge(b.dS.h, rec["ld"])
                    rec["sq"] = b.eS.tick(s.activation(sqs[:], xt[:], AF.Square))

                def v_ln(v):
                    v.wait_ge(b.dS.h, rec["ld"])
                    v.wait_ge(b.eS.h, rec["sq"])
                    x3 = xt[:].rearrange("p (c d) -> p c d", d=1024)
                    s3 = sqs[:].rearrange("p (c d) -> p c d", d=1024)
                    v.tensor_reduce(lsum[:].unsqueeze(2), x3, AX.X, ALU.add)
                    b.gap(v, b.eV, v.tensor_reduce(lssq[:].unsqueeze(2), s3,
                                                   AX.X, ALU.add))
                    b.gap(v, b.eV, v.scalar_tensor_tensor(
                        ltmp[:], lsum[:], 1.0 / 1024.0, lsum[:], ALU.mult,
                        ALU.mult))
                    b.gap(v, b.eV, v.tensor_tensor(lvar[:], lssq[:], ltmp[:],
                                                   ALU.subtract))
                    rec["ve"] = b.eV.tick(v.tensor_scalar(
                        lvar[:], lvar[:], 1.0 / 1024.0, 1e-5, ALU.mult, ALU.add))
                    v.wait_ge(b.eV.h, rec["ve"])

                def s_sqrt(s):
                    s.wait_ge(b.eV.h, rec["ve"])
                    s.activation(lsrt[:], lvar[:], AF.Ln)
                    rec["srt"] = b.eS.tick(s.activation(lrstd[:], lsrt[:],
                                                        AF.Exp, scale=-0.5))

                def v_xn(v):
                    v.wait_ge(b.eS.h, rec["srt"])
                    b.gap(v, b.eV, v.tensor_scalar(lmean[:], lsum[:],
                                                   1.0 / 1024.0, None, ALU.mult))
                    for ch in range(4):
                        v.tensor_scalar(xnt[:, 1024 * ch:1024 * (ch + 1)],
                                        xt[:, 1024 * ch:1024 * (ch + 1)],
                                        lmean[:, ch:ch + 1], lrstd[:, ch:ch + 1],
                                        ALU.subtract, ALU.mult)
                    rec["xn"] = b.eV.tick(v.tensor_copy(xnb[:], xnt[:]))

                blk.sync(sy_ld)
                blk.scalar(s_sq)
                blk.vector(v_ln)
                blk.scalar(s_sqrt)
                blk.vector(v_xn)
                for qj in qjs:
                    blk.vector(qj.v1)
                    blk.scalar(qj.s)
                    blk.gpsimd(qj.g)
                    blk.vector(qj.v2)
                qj_done = b.eV.n

                last_copy = {}
                for which, src, dsts, wkey in (("xn", xnb, xnfm, "xn"),
                                               ("xq", xqt, xqfm, None)):
                    for kc in range(8):
                        par = kc % 2

                        def t_grp(t, src=src, kc=kc, par=par, wkey=wkey):
                            if wkey:
                                t.wait_ge(b.eV.h, rec[wkey])
                            else:
                                t.wait_ge(b.eV.h, qj_done)
                            prev = last_copy.get(par)
                            if prev:
                                t.wait_ge(b.eV.h, prev)
                            for ch in range(4):
                                ins = t.transpose(
                                    pst[par][:, 128 * ch:128 * (ch + 1)],
                                    src[:, 1024 * ch + 128 * kc:
                                        1024 * ch + 128 * (kc + 1)],
                                    b.identb[:])
                            rec[("t", src.name, kc)] = b.eT.tick(ins)

                        def v_grp(v, src=src, dsts=dsts, kc=kc, par=par):
                            v.wait_ge(b.eT.h, rec[("t", src.name, kc)])
                            ins = v.tensor_copy(dsts[kc][:], pst[par][:])
                            last_copy[par] = b.eV.tick(ins)
                            rec[("fm", dsts[kc].name)] = last_copy[par]

                        blk.tensor(t_grp)
                        blk.vector(v_grp)
                xn_fm_done = b.eV.n

                col = None
                last_evac = {}
                for mc in range(8):
                    par_ = mc % 2
                    pq, pk, pv = pq_[par_], pk_[par_], pv_[par_]

                    def t_qkv(t, mc=mc, pq=pq, pk=pk, pv=pv):
                        if mc == 0:
                            t.wait_ge(b.dG.h, w_ld)
                            for k in range(8):
                                t.wait_ge(b.eV.h, rec[("fm", xnfm[k].name)])
                        for pp, W in ((pq, Wq_sb), (pk, Wk_sb), (pv, Wv_sb)):
                            prev = last_evac.get(pp.name)
                            if prev:
                                t.wait_ge(b.eV.h, prev)
                            for k in range(8):
                                ins = t.matmul(pp[:],
                                               W[k][:, 128 * mc:128 * (mc + 1)],
                                               xnfm[k][:], start=(k == 0),
                                               stop=(k == 7))
                            rec[("mm", pp.name, mc)] = b.eT.tick(ins)

                    def v_fq(v, mc=mc, pq=pq):
                        v.wait_ge(b.eT.h, rec[("mm", pq.name, mc)])
                        rec[("faq", mc)] = b.eV.tick(v.tensor_scalar(
                            fa[:], pq[:], b.bq_c[:, mc:mc + 1], 0.0, ALU.add,
                            ALU.min))

                    def s_fq(s, mc=mc):
                        s.wait_ge(b.eV.h, rec[("faq", mc)])
                        rec[("feq", mc)] = b.eS.tick(
                            s.activation(fe[:], fa[:], AF.Exp))

                    def v_q2(v, mc=mc, pq=pq, pk=pk):
                        v.tensor_scalar(fr[:], pq[:], b.bq_c[:, mc:mc + 1], 0.0,
                                        ALU.add, ALU.max)
                        v.wait_ge(b.eS.h, rec[("feq", mc)])
                        ins = v.tensor_tensor(qsto[mc][:], fr[:], fe[:], ALU.add)
                        rec[("qf", mc)] = b.eV.tick(ins)
                        last_evac[pq.name] = rec[("qf", mc)]
                        v.wait_ge(b.eT.h, rec[("mm", pk.name, mc)])
                        rec[("fak", mc)] = b.eV.tick(v.tensor_scalar(
                            fa[:], pk[:], b.bk_c[:, mc:mc + 1], 0.0, ALU.add,
                            ALU.min))

                    def s_fk(s, mc=mc):
                        s.wait_ge(b.eV.h, rec[("fak", mc)])
                        rec[("fek", mc)] = b.eS.tick(
                            s.activation(fe[:], fa[:], AF.Exp))

                    def v_k2(v, mc=mc, bb=bb, pk=pk, pv=pv):
                        v.tensor_scalar(fr[:], pk[:], b.bk_c[:, mc:mc + 1], 0.0,
                                        ALU.add, ALU.max)
                        v.wait_ge(b.eS.h, rec[("fek", mc)])
                        v.tensor_tensor(kf[:], fr[:], fe[:], ALU.add)
                        v.wait_ge(b.eT.h, rec[("mm", pv.name, mc)])
                        v.tensor_scalar(vp[:], pv[:], b.bv_c[:, mc:mc + 1], None,
                                        ALU.add)
                        ins = v.tensor_tensor(zz[:], kf[:], vp[:], ALU.mult)
                        n = b.eV.tick(ins)
                        last_evac[pk.name] = n
                        last_evac[pv.name] = n
                        cc = mc * 2 + bb
                        b.gap(v, b.eV, v.tensor_reduce(lred[:], zz[:], AX.X,
                                                       ALU.add))
                        b.gap(v, b.eV, v.tensor_tensor(
                            b.kvacc[:, cc:cc + 1], b.kvacc[:, cc:cc + 1],
                            lred[:], ALU.add))
                        b.gap(v, b.eV, v.tensor_reduce(lred[:], kf[:], AX.X,
                                                       ALU.add))
                        b.gap(v, b.eV, v.tensor_tensor(
                            b.ksacc[:, cc:cc + 1], b.ksacc[:, cc:cc + 1],
                            lred[:], ALU.add))

                    blk.tensor(t_qkv)
                    blk.vector(v_fq)
                    blk.scalar(s_fq)
                    blk.vector(v_q2)
                    blk.scalar(s_fk)
                    blk.vector(v_k2)

                def sy_st(sy, t_i=t_i):
                    for mc in range(8):
                        sy.wait_ge(b.eV.h, rec[("qf", mc)])
                        b.dS.dtick(sy.dma_start(
                            out=b.scr["s_q0"][128 * mc:128 * (mc + 1),
                                              TT * t_i:TT * (t_i + 1)],
                            in_=qsto[mc][:]))
                    for kc in range(8):
                        sy.wait_ge(b.eV.h, rec[("fm", xqfm[kc].name)])
                        b.dS.dtick(sy.dma_start(
                            out=b.scr["s_xq"][128 * kc:128 * (kc + 1),
                                              TT * t_i:TT * (t_i + 1)],
                            in_=xqfm[kc][:]))

                blk.sync(sy_st)
        b.dbg_store("kvacc", b.kvacc[:], [])
        b.dbg_store("ksacc", b.ksacc[:], [])
    b.barrier()


def phase_p0c1(b):
    nc = b.nc
    NTS = [(0, 512), (512, 1024), (1024, 1536), (1536, 2048), (2048, 2560),
           (2560, 2816)]
    with ExitStack() as st:
        sbn = lambda n, s, d=F32: st.enter_context(nc.sbuf_tensor(n, s, d))
        wgq_sb = [sbn(f"c1_wg{k}", [128, I_FFN], BF16) for k in range(8)]
        wuq_sb = [sbn(f"c1_wu{k}", [128, I_FFN], BF16) for k in range(8)]
        xqt = [sbn(f"c1_xq{k}", [128, 512], BF16) for k in range(8)]
        sg = sbn("c1_sg", [128, 512])
        ht = sbn("c1_h", [128, I_FFN])
        hqt = sbn("c1_hq", [128, I_FFN], BF16)
        bands = [sbn(f"c1_bd{k}", [128, 512], BF16) for k in range(NI)]
        pg = [st.enter_context(nc.psum_tensor(f"c1_pg{j}", [128, 512], F32))
              for j in range(2)]
        pu = [st.enter_context(nc.psum_tensor(f"c1_pu{j}", [128, 512], F32))
              for j in range(2)]
        pst = [st.enter_context(nc.psum_tensor(f"c1_ps{j}", [128, 512], BF16))
               for j in range(2)]

        with nc.Block() as blk:
            def sy_w(sy):
                b.bwait(sy)
                for tiles, nm in ((wgq_sb, "s_wgq"), (wuq_sb, "s_wuq")):
                    for k in range(8):
                        b.dS.dtick(sy.dma_start(
                            out=tiles[k][:],
                            in_=b.scr[nm][128 * k:128 * (k + 1), :]))
            blk.sync(sy_w)
        w_ld = b.dS.n

        for t_i in range(NT):
            rec = {}
            with nc.Block() as blk:
                def sy_ld(sy, t_i=t_i):
                    sy.wait_ge(b.eT.h, b.eT.n)
                    for k in range(8):
                        b.dS.dtick(sy.dma_start(
                            out=xqt[k][:],
                            in_=b.scr["s_xq"][128 * k:128 * (k + 1),
                                              TT * t_i:TT * (t_i + 1)]))
                    rec["ld"] = b.dS.n
                blk.sync(sy_ld)

                for tc in range(4):
                    for ni, (n0, n1) in enumerate(NTS):
                        par = ni % 2

                        def t_mm(t, tc=tc, ni=ni, n0=n0, n1=n1, t_i=t_i,
                                 par=par):
                            if t_i == 0 and tc == 0 and ni == 0:
                                t.wait_ge(b.dS.h, w_ld)
                            t.wait_ge(b.dS.h, rec["ld"])
                            prev = rec.get(("hm_p", par))
                            if prev:
                                t.wait_ge(b.eV.h, prev)
                            for k in range(8):
                                t.matmul(pg[par][:, 0:n1 - n0],
                                         xqt[k][:, 128 * tc:128 * (tc + 1)],
                                         wgq_sb[k][:, n0:n1], start=(k == 0),
                                         stop=False)
                            for k in range(8):
                                ins = t.matmul(pu[par][:, 0:n1 - n0],
                                               xqt[k][:, 128 * tc:128 * (tc + 1)],
                                               wuq_sb[k][:, n0:n1],
                                               start=(k == 0), stop=(k == 7))
                            rec[("mm", tc, ni)] = b.eT.tick(ins)

                        def s_si(s, tc=tc, ni=ni, n0=n0, n1=n1, par=par):
                            s.wait_ge(b.eT.h, rec[("mm", tc, ni)])
                            prev = rec.get(("hm_p", 1 - par))
                            if prev:
                                s.wait_ge(b.eV.h, prev)
                            rec[("si", tc, ni)] = b.eS.tick(s.activation(
                                sg[:, 0:n1 - n0], pg[par][:, 0:n1 - n0],
                                AF.Silu))

                        def v_h(v, tc=tc, ni=ni, n0=n0, n1=n1, par=par):
                            v.wait_ge(b.eS.h, rec[("si", tc, ni)])
                            ins = v.tensor_tensor(ht[:, n0:n1],
                                                  sg[:, 0:n1 - n0],
                                                  pu[par][:, 0:n1 - n0],
                                                  ALU.mult)
                            rec[("hm", tc, ni)] = b.eV.tick(ins)
                            rec[("hm_p", par)] = rec[("hm", tc, ni)]

                        blk.tensor(t_mm)
                        blk.scalar(s_si)
                        blk.vector(v_h)

                    qj = b.quant_job(ht[:], hqt[:], I_FFN)
                    blk.vector(qj.v1)
                    blk.scalar(qj.s)
                    blk.gpsimd(qj.g)
                    prev_tr = rec.get(("tr", tc - 1, 20))
                    if prev_tr:
                        blk.vector(lambda v, p=prev_tr: v.wait_ge(b.eT.h, p))
                    blk.vector(qj.v2)

                    for gi in range(0, NI, 4):
                        kks = list(range(gi, min(gi + 4, NI)))
                        par = (gi // 4) % 2

                        def t_tr(t, kks=kks, par=par, tc=tc, qj=qj):
                            t.wait_ge(b.eV.h, qj.done)
                            prev = rec.get(("bc", par))
                            if prev:
                                t.wait_ge(b.eV.h, prev)
                            for qi, kk in enumerate(kks):
                                ins = t.transpose(
                                    pst[par][:, 128 * qi:128 * (qi + 1)],
                                    hqt[:, 128 * kk:128 * (kk + 1)], b.identb[:])
                            rec[("tr", tc, gi)] = b.eT.tick(ins)

                        def v_bc(v, kks=kks, par=par, tc=tc, gi=gi):
                            v.wait_ge(b.eT.h, rec[("tr", tc, gi)])
                            for qi, kk in enumerate(kks):
                                ins = v.tensor_copy(
                                    bands[kk][:, 128 * tc:128 * (tc + 1)],
                                    pst[par][:, 128 * qi:128 * (qi + 1)])
                            rec[("bc", par)] = b.eV.tick(ins)
                            rec[("bcd", tc, gi)] = rec[("bc", par)]

                        blk.tensor(t_tr)
                        blk.vector(v_bc)

                def sy_st(sy, t_i=t_i):
                    sy.wait_ge(b.eV.h, b.eV.n)
                    for kk in range(NI):
                        b.dS.dtick(sy.dma_start(
                            out=b.scr["s_hq"][128 * kk:128 * (kk + 1),
                                              TT * t_i:TT * (t_i + 1)],
                            in_=bands[kk][:]))
                blk.sync(sy_st)
    b.barrier()


def phase_p0c2(b):
    nc = b.nc
    with ExitStack() as st:
        sbn = lambda n, s, d=F32: st.enter_context(nc.sbuf_tensor(n, s, d))
        wdq_sb = [sbn(f"c2_wd{k}", [128, D], BF16) for k in range(NI)]
        hqt = [sbn(f"c2_hq{k}", [128, 512], BF16) for k in range(NI)]
        fft = [sbn(f"c2_ff{j}", [128, 512]) for j in range(2)]
        pd = [st.enter_context(nc.psum_tensor(f"c2_pd{j}", [128, 512], F32))
              for j in range(2)]

        with nc.Block() as blk:
            def sy_w(sy):
                b.bwait(sy)
                for k in range(NI):
                    b.dS.dtick(sy.dma_start(
                        out=wdq_sb[k][:],
                        in_=b.scr["s_wdq"][128 * k:128 * (k + 1), :]))
            blk.sync(sy_w)
        w_ld = b.dS.n

        for t_i in range(NT):
            rec = {}
            with nc.Block() as blk:
                def sy_ld(sy, t_i=t_i):
                    sy.wait_ge(b.eT.h, b.eT.n)
                    for k in range(NI):
                        b.dS.dtick(sy.dma_start(
                            out=hqt[k][:],
                            in_=b.scr["s_hq"][128 * k:128 * (k + 1),
                                              TT * t_i:TT * (t_i + 1)]))
                    rec["ld"] = b.dS.n
                blk.sync(sy_ld)

                for mc in range(8):
                    par = mc % 2

                    def t_mm(t, mc=mc, par=par, t_i=t_i):
                        if t_i == 0 and mc == 0:
                            t.wait_ge(b.dS.h, w_ld)
                        t.wait_ge(b.dS.h, rec["ld"])
                        prev = rec.get(("ev", par))
                        if prev:
                            t.wait_ge(b.eV.h, prev)
                        for k in range(NI):
                            ins = t.matmul(pd[par][:],
                                           wdq_sb[k][:, 128 * mc:128 * (mc + 1)],
                                           hqt[k][:], start=(k == 0),
                                           stop=(k == NI - 1))
                        rec[("mm", mc)] = b.eT.tick(ins)

                    def v_ev(v, mc=mc, par=par):
                        v.wait_ge(b.eT.h, rec[("mm", mc)])
                        prevst = rec.get(("st", par))
                        if prevst:
                            v.wait_ge(b.dS.h, prevst)
                        ins = v.tensor_copy(fft[par][:], pd[par][:])
                        rec[("ev", par)] = b.eV.tick(ins)

                    def sy_one(sy, mc=mc, par=par, t_i=t_i):
                        sy.wait_ge(b.eV.h, rec[("ev", par)])
                        b.dS.dtick(sy.dma_start(
                            out=b.scr["s_ffn0"][128 * mc:128 * (mc + 1),
                                                TT * t_i:TT * (t_i + 1)],
                            in_=fft[par][:]))
                        rec[("st", par)] = b.dS.n

                    blk.tensor(t_mm)
                    blk.vector(v_ev)
                    blk.sync(sy_one)
    b.barrier()


def phase_p0b(b):
    nc = b.nc
    with ExitStack() as st:
        sbn = lambda n, s, d=F32: st.enter_context(nc.sbuf_tensor(n, s, d))
        Wo_sb = [sbn(f"b_wo{k}", [128, D], BF16) for k in range(8)]
        qt = [sbn(f"b_q{k}", [128, 512]) for k in range(8)]
        fft = [sbn(f"b_ff{k}", [128, 512]) for k in range(8)]
        xt = sbn("b_xt", [128, 4096])
        x1t = sbn("b_x1", [128, 4096])
        vl = [sbn(f"b_vl{k}", [128, 512], BF16) for k in range(8)]
        t1s = [sbn(f"b_t1{k}", [128, 512]) for k in range(8)]
        z8 = [sbn(f"b_z8{k}", [128, 512]) for k in range(8)]
        uu = sbn("b_uu", [128, 512])
        go = sbn("b_go", [128, 512])
        af = sbn("b_af", [128, 512])
        po = [st.enter_context(nc.psum_tensor(f"b_po{j}", [128, 512], F32))
              for j in range(2)]
        ptf = [st.enter_context(nc.psum_tensor(f"b_ptf{j}", [128, 512], F32))
               for j in range(2)]

        with nc.Block() as blk:
            def g_w(g):
                b.bwait(g)
                for k in range(8):
                    b.dG.dtick(g.dma_start(
                        out=Wo_sb[k][:],
                        in_=b.in_ap["L0_attn_Wo"][128 * k:128 * (k + 1), :]))
            blk.gpsimd(g_w)
        w_ld = b.dG.n

        for t_i in range(NT):
            bb = t_i // 4
            rec = {}
            with nc.Block() as blk:
                def sy_ld(sy, t_i=t_i):
                    sy.wait_ge(b.eV.h, b.eV.n)
                    sy.wait_ge(b.eS.h, b.eS.n)
                    sy.wait_ge(b.eT.h, b.eT.n)
                    for k in range(8):
                        b.dS.dtick(sy.dma_start(
                            out=qt[k][:],
                            in_=b.scr["s_q0"][128 * k:128 * (k + 1),
                                              TT * t_i:TT * (t_i + 1)]))
                    for k in range(8):
                        b.dS.dtick(sy.dma_start(
                            out=fft[k][:],
                            in_=b.scr["s_ffn0"][128 * k:128 * (k + 1),
                                                TT * t_i:TT * (t_i + 1)]))
                    for ch in range(4):
                        r0 = TT * t_i + 128 * ch
                        b.dS.dtick(sy.dma_start(
                            out=xt[:, 1024 * ch:1024 * (ch + 1)],
                            in_=b.in_ap["x"][r0:r0 + 128, :]))
                    rec["ld"] = b.dS.n
                blk.sync(sy_ld)

                def v_t1all(v, bb=bb):
                    v.wait_ge(b.dS.h, rec["ld"])
                    v.wait_ge(b.dG.h, w_ld)
                    for kc in range(8):
                        cc = kc * 2 + bb
                        ins = v.tensor_scalar(
                            t1s[kc][:], qt[kc][:], b.ksacc[:, cc:cc + 1], 1e-6,
                            ALU.mult, ALU.add)
                    rec["t1"] = b.eV.tick(ins)

                def s_zall(s):
                    s.wait_ge(b.eV.h, rec["t1"])
                    for kc in range(8):
                        s.activation(z8[kc][:], t1s[kc][:], AF.Ln)
                    for kc in range(8):
                        ins = s.activation(t1s[kc][:], z8[kc][:], AF.Exp,
                                           scale=-1.0)
                    rec["z"] = b.eS.tick(ins)

                def v_vlall(v, bb=bb):
                    v.wait_ge(b.eS.h, rec["z"])
                    for kc in range(8):
                        cc = kc * 2 + bb
                        v.tensor_scalar(uu[:], qt[kc][:], b.kvacc[:, cc:cc + 1],
                                        None, ALU.mult)
                        rec[("vl", kc)] = b.eV.tick(
                            v.tensor_tensor(vl[kc][:], uu[:], t1s[kc][:],
                                            ALU.mult))

                blk.vector(v_t1all)
                blk.scalar(s_zall)
                blk.vector(v_vlall)

                for mc in range(8):
                    par = mc % 2

                    def t_mm(t, mc=mc, par=par):
                        for k in range(8):
                            t.wait_ge(b.eV.h, rec[("vl", k)])
                        prev = rec.get(("go", par))
                        if prev:
                            t.wait_ge(b.eS.h, prev)
                        for k in range(8):
                            ins = t.matmul(po[par][:],
                                           Wo_sb[k][:, 128 * mc:128 * (mc + 1)],
                                           vl[k][:], start=(k == 0),
                                           stop=(k == 7))
                        rec[("mm", mc)] = b.eT.tick(ins)

                    def s_go(s, mc=mc, par=par):
                        s.wait_ge(b.eT.h, rec[("mm", mc)])
                        prev = rec.get(("af", 1 - par))
                        if prev:
                            s.wait_ge(b.eV.h, prev)
                        rec[("go", par)] = b.eS.tick(s.activation(
                            go[:], po[par][:], AF.Gelu_apprx_tanh,
                            bias=b.bo_c[:, mc:mc + 1]))

                    def v_af(v, mc=mc, par=par):
                        v.wait_ge(b.eS.h, rec[("go", par)])
                        prev = rec.get(("tr", mc - 1))
                        if prev:
                            v.wait_ge(b.eT.h, prev)
                        rec[("af", par)] = b.eV.tick(
                            v.tensor_tensor(af[:], go[:], fft[mc][:], ALU.add))

                    def t_tr(t, mc=mc, par=par):
                        t.wait_ge(b.eV.h, rec[("af", par)])
                        prev = rec.get(("x1", par))
                        if prev:
                            t.wait_ge(b.eV.h, prev)
                        for tc in range(4):
                            ins = t.transpose(
                                ptf[par][:, 128 * tc:128 * (tc + 1)],
                                af[:, 128 * tc:128 * (tc + 1)], b.identf[:])
                        rec[("tr", mc)] = b.eT.tick(ins)

                    def v_x1(v, mc=mc, par=par):
                        v.wait_ge(b.eT.h, rec[("tr", mc)])
                        for tc in range(4):
                            col = 1024 * tc + 128 * mc
                            ins = v.tensor_tensor(
                                x1t[:, col:col + 128],
                                ptf[par][:, 128 * tc:128 * (tc + 1)],
                                xt[:, col:col + 128], ALU.add)
                        rec[("x1", par)] = b.eV.tick(ins)

                    blk.tensor(t_mm)
                    blk.scalar(s_go)
                    blk.vector(v_af)
                    blk.tensor(t_tr)
                    blk.vector(v_x1)

                def sy_st(sy, t_i=t_i):
                    sy.wait_ge(b.eV.h, b.eV.n)
                    for ch in range(4):
                        r0 = TT * t_i + 128 * ch
                        b.dS.dtick(sy.dma_start(
                            out=b.scr["s_x1"][r0:r0 + 128, :],
                            in_=x1t[:, 1024 * ch:1024 * (ch + 1)]))
                blk.sync(sy_st)
    b.barrier()


def phase_p1a(b):
    nc = b.nc
    with ExitStack() as st:
        sbn = lambda n, s, d=F32: st.enter_context(nc.sbuf_tensor(n, s, d))
        x1t = sbn("a1_x1", [128, 4096])
        xnt = sbn("a1_xn", [128, 4096])
        sqs = xnt
        xnb = sbn("a1_xnb", [128, 4096], BF16)
        xnfm = [sbn(f"a1_fm{k}", [128, 512], BF16) for k in range(8)]
        Wa_sb = [sbn(f"a1_wa{k}", [128, 3 * D], BF16) for k in range(8)]
        battn_row = sbn("a1_brow", [1, 3 * D], BF16)
        qke = [sbn(f"a1_qke{j}", [128, 512], BF16) for j in range(2)]
        vtile = sbn("a1_vt", [128, NH * 65], BF16)
        lsum = sbn("a1_lsum", [128, 4])
        lssq = sbn("a1_lssq", [128, 4])
        ltmp = sbn("a1_ltmp", [128, 4])
        lvar = sbn("a1_lvar", [128, 4])
        lsrt = sbn("a1_lsrt", [128, 4])
        lrstd = sbn("a1_lrstd", [128, 4])
        lmean = sbn("a1_lmean", [128, 4])
        lred = sbn("a1_lred", [128, 1])
        pst = [st.enter_context(nc.psum_tensor(f"a1_ps{j}", [128, 512], BF16))
               for j in range(2)]
        pqk = [st.enter_context(nc.psum_tensor(f"a1_pqk{j}", [128, 512], F32))
               for j in range(2)]
        pv = [st.enter_context(nc.psum_tensor(f"a1_pv{j}", [128, 512], F32))
              for j in range(2)]

        with nc.Block() as blk:
            def g_w(g):
                b.bwait(g)
                for k in range(8):
                    b.dG.dtick(g.dma_start(
                        out=Wa_sb[k][:],
                        in_=b.in_ap["L1_attn_W_attn"][128 * k:128 * (k + 1), :]))
                b.dG.dtick(g.dma_start(
                    out=battn_row[:], in_=b.in_ap["L1_attn_b_attn"].unsqueeze(0)))
            blk.gpsimd(g_w)
        w_ld = b.dG.n

        for t_i in range(NT):
            bb = t_i // 4
            rec = {}
            with nc.Block() as blk:
                def sy_ld(sy, t_i=t_i):
                    sy.wait_ge(b.eV.h, b.eV.n)
                    sy.wait_ge(b.eS.h, b.eS.n)
                    sy.wait_ge(b.eT.h, b.eT.n)
                    sy.wait_ge(b.eG.h, b.eG.n)
                    for ch in range(4):
                        r0 = TT * t_i + 128 * ch
                        b.dS.dtick(sy.dma_start(
                            out=x1t[:, 1024 * ch:1024 * (ch + 1)],
                            in_=b.scr["s_x1"][r0:r0 + 128, :]))
                    rec["ld"] = b.dS.n

                def s_sq(s):
                    s.wait_ge(b.dS.h, rec["ld"])
                    rec["sq"] = b.eS.tick(s.activation(sqs[:], x1t[:], AF.Square))

                def v_ln(v):
                    v.wait_ge(b.dS.h, rec["ld"])
                    v.wait_ge(b.eS.h, rec["sq"])
                    x3 = x1t[:].rearrange("p (c d) -> p c d", d=1024)
                    s3 = sqs[:].rearrange("p (c d) -> p c d", d=1024)
                    v.tensor_reduce(lsum[:].unsqueeze(2), x3, AX.X, ALU.add)
                    b.gap(v, b.eV, v.tensor_reduce(lssq[:].unsqueeze(2), s3,
                                                   AX.X, ALU.add))
                    b.gap(v, b.eV, v.scalar_tensor_tensor(
                        ltmp[:], lsum[:], 1.0 / 1024.0, lsum[:], ALU.mult,
                        ALU.mult))
                    b.gap(v, b.eV, v.tensor_tensor(lvar[:], lssq[:], ltmp[:],
                                                   ALU.subtract))
                    rec["ve"] = b.eV.tick(v.tensor_scalar(
                        lvar[:], lvar[:], 1.0 / 1024.0, 1e-5, ALU.mult, ALU.add))
                    v.wait_ge(b.eV.h, rec["ve"])

                def s_sqrt(s):
                    s.wait_ge(b.eV.h, rec["ve"])
                    s.activation(lsrt[:], lvar[:], AF.Ln)
                    rec["srt"] = b.eS.tick(s.activation(lrstd[:], lsrt[:],
                                                        AF.Exp, scale=-0.5))

                def v_xn(v):
                    v.wait_ge(b.eS.h, rec["srt"])
                    b.gap(v, b.eV, v.tensor_scalar(lmean[:], lsum[:],
                                                   1.0 / 1024.0, None, ALU.mult))
                    for ch in range(4):
                        v.tensor_scalar(xnt[:, 1024 * ch:1024 * (ch + 1)],
                                        x1t[:, 1024 * ch:1024 * (ch + 1)],
                                        lmean[:, ch:ch + 1], lrstd[:, ch:ch + 1],
                                        ALU.subtract, ALU.mult)
                    rec["xn"] = b.eV.tick(v.tensor_copy(xnb[:], xnt[:]))

                def sy_xn(sy, t_i=t_i):
                    sy.wait_ge(b.eV.h, rec["xn"])
                    for ch in range(4):
                        r0 = TT * t_i + 128 * ch
                        b.dS.dtick(sy.dma_start(
                            out=b.scr["s_xn1"][r0:r0 + 128, :],
                            in_=xnt[:, 1024 * ch:1024 * (ch + 1)]))
                        b.dS.dtick(sy.dma_start(
                            out=b.scr["s_xn1b"][r0:r0 + 128, :],
                            in_=xnb[:, 1024 * ch:1024 * (ch + 1)]))

                blk.sync(sy_ld)
                blk.scalar(s_sq)
                blk.vector(v_ln)
                blk.scalar(s_sqrt)
                blk.vector(v_xn)
                blk.sync(sy_xn)

                last_copy = {}
                for kc in range(8):
                    par = kc % 2

                    def t_grp(t, kc=kc, par=par):
                        t.wait_ge(b.eV.h, rec["xn"])
                        prev = last_copy.get(par)
                        if prev:
                            t.wait_ge(b.eV.h, prev)
                        for ch in range(4):
                            ins = t.transpose(
                                pst[par][:, 128 * ch:128 * (ch + 1)],
                                xnb[:, 1024 * ch + 128 * kc:
                                    1024 * ch + 128 * (kc + 1)],
                                b.identb[:])
                        rec[("t", kc)] = b.eT.tick(ins)

                    def v_grp(v, kc=kc, par=par):
                        v.wait_ge(b.eT.h, rec[("t", kc)])
                        ins = v.tensor_copy(xnfm[kc][:], pst[par][:])
                        last_copy[par] = b.eV.tick(ins)
                        rec[("fm", kc)] = last_copy[par]

                    blk.tensor(t_grp)
                    blk.vector(v_grp)

                for mc in range(16):
                    par = mc % 2
                    acc = b.qrm if mc < 8 else b.krm
                    cc = (mc % 8) * 2 + bb

                    def t_mm(t, mc=mc, par=par):
                        if mc == 0:
                            t.wait_ge(b.dG.h, w_ld)
                            for k in range(8):
                                t.wait_ge(b.eV.h, rec[("fm", k)])
                        prev = rec.get(("qkd", par))
                        if prev:
                            t.wait_ge(b.dS.h, prev)
                        prev2 = rec.get(("rm", par))
                        if prev2:
                            t.wait_ge(b.eV.h, prev2)
                        for k in range(8):
                            ins = t.matmul(pqk[par][:],
                                           Wa_sb[k][:, 128 * mc:128 * (mc + 1)],
                                           xnfm[k][:], start=(k == 0),
                                           stop=(k == 7))
                        rec[("mm", mc)] = b.eT.tick(ins)

                    def v_ev(v, mc=mc, par=par, acc=acc, cc=cc):
                        v.wait_ge(b.eT.h, rec[("mm", mc)])
                        ins = v.tensor_scalar(qke[par][:], pqk[par][:],
                                              b.battn_c[:, mc:mc + 1], None,
                                              ALU.add)
                        rec[("rm", par)] = b.eV.tick(ins)

                    def sy_qk(sy, mc=mc, par=par, t_i=t_i):
                        sy.wait_ge(b.eV.h, rec[("rm", par)])
                        b.dS.dtick(sy.dma_start(
                            out=b.scr["s_qk"][128 * mc:128 * (mc + 1),
                                              TT * t_i:TT * (t_i + 1)],
                            in_=qke[par][:]))
                        rec[("qkd", par)] = b.dS.n

                    blk.tensor(t_mm)
                    blk.vector(v_ev)
                    blk.sync(sy_qk)

                def v_m1(v):
                    prev = rec.get("vt_store")
                    ins = v.memset(vtile[:], 1.0)
                    rec["vms"] = b.eV.tick(ins)

                blk.vector(v_m1)
                for tc in range(4):
                    for half in range(2):
                        par = half

                        def t_v(t, tc=tc, half=half, par=par):
                            prev = rec.get(("vev", par))
                            if prev:
                                t.wait_ge(b.eV.h, prev)
                            for k in range(8):
                                t.matmul(pv[par][:],
                                         xnfm[k][:, 128 * tc:128 * (tc + 1)],
                                         Wa_sb[k][:, 2 * D + 512 * half:
                                                   2 * D + 512 * (half + 1)],
                                         start=(k == 0), stop=False)
                            ins = t.matmul(
                                pv[par][:], b.onetokb[:],
                                battn_row[:, 2 * D + 512 * half:
                                          2 * D + 512 * (half + 1)],
                                start=False, stop=True)
                            rec[("vmm", tc, half)] = b.eT.tick(ins)

                        def v_vev(v, tc=tc, half=half, par=par):
                            v.wait_ge(b.eV.h, rec["vms"])
                            v.wait_ge(b.eT.h, rec[("vmm", tc, half)])
                            src3 = pv[par][:].rearrange("p (h j) -> p h j", j=64)
                            dst3 = vtile[:, 65 * 8 * half:65 * 8 * (half + 1)] \
                                .rearrange("p (h j) -> p h j", j=65)[:, :, 0:64]
                            ins = v.tensor_copy(dst3, src3)
                            rec[("vev", par)] = b.eV.tick(ins)

                        blk.tensor(t_v)
                        blk.vector(v_vev)

                    def sy_v(sy, tc=tc, t_i=t_i):
                        sy.wait_ge(b.eV.h, rec[("vev", 0)])
                        sy.wait_ge(b.eV.h, rec[("vev", 1)])
                        r0 = TT * t_i + 128 * tc
                        b.dS.dtick(sy.dma_start(
                            out=b.scr["s_v"][r0:r0 + 128, :], in_=vtile[:]))
                        rec["vt_store"] = b.dS.n

                    blk.sync(sy_v)

                    if tc < 3:
                        def v_m2(v, tc=tc):
                            v.wait_ge(b.dS.h, rec["vt_store"])
                            rec["vms"] = b.eV.tick(v.memset(vtile[:], 1.0))
                        blk.vector(v_m2)
    b.dbg_store("qrm", b.qrm[:], [])
    b.dbg_store("krm", b.krm[:], [])
    b.barrier()


def phase_p1b(b):
    nc = b.nc
    with ExitStack() as st:
        sbn = lambda n, s, d=F32: st.enter_context(nc.sbuf_tensor(n, s, d))
        qh = [sbn(f"f_qh{j}", [64, S], BF16) for j in range(2)]
        kh = [sbn(f"f_kh{j}", [64, S], BF16) for j in range(2)]
        vb = [sbn(f"f_vb{k}", [128, NH * 65], BF16) for k in range(16)]
        Osb = [sbn(f"f_o{k}", [128, S], BF16) for k in range(8)]
        Pt = [sbn(f"f_p{j}", [128, 512], BF16) for j in range(4)]
        Lrow = sbn("f_l", [65, S])
        RL = sbn("f_rl", [65, S])
        ones65 = sbn("f_o65", [65, 128])
        at = [sbn(f"f_at{j}", [128, 512]) for j in range(2)]
        a1t = sbn("f_a1", [128, 4096])
        Wp_sb = [sbn(f"f_wp{k}", [128, D], BF16) for k in range(8)]
        pS = [st.enter_context(nc.psum_tensor(f"f_pS{j}", [128, 512], F32))
              for j in range(4)]
        pO = st.enter_context(nc.psum_tensor("f_pO", [128, 512], F32))
        pB = [st.enter_context(nc.psum_tensor(f"f_pB{j}", [128, 512], F32))
              for j in range(2)]

        rec = {}
        with nc.Block() as blk:
            def g_w(g):
                b.bwait(g)
                for k in range(8):
                    b.dG.dtick(g.dma_start(
                        out=Wp_sb[k][:],
                        in_=b.in_ap["L1_attn_W_proj"][128 * k:128 * (k + 1), :]))
            blk.gpsimd(g_w)

            def v_pre(v):
                b.bwait(v)
                ins = v.memset(ones65[:], 1.0)
                b.eV.tick(ins)
            blk.vector(v_pre)

        for bb in range(2):
            with nc.Block() as blk:
                def sy_vb(sy, bb=bb):
                    sy.wait_ge(b.eT.h, b.eT.n)
                    for k in range(16):
                        r0 = S * bb + 128 * k
                        b.dS.dtick(sy.dma_start(
                            out=vb[k][:], in_=b.scr["s_v"][r0:r0 + 128, :]))
                    rec["vb"] = b.dS.n
                blk.sync(sy_vb)

            for h in range(NH):
                mcol = (h % 2) * 16 + (h // 2) * 2 + bb
                ch = h // 2
                rb = 64 * (h % 2)
                hp = h % 2
                eT0, eS0, eV0 = b.eT.n, b.eS.n, b.eV.n
                tS, tE, tM, tPV, tOev, tOsc, tBC = {}, {}, {}, {}, {}, {}, {}
                nT, nS, nV = eT0, eS0, eV0
                for qt in range(4):
                    nkc = 4 * qt + 4
                    for kc in range(min(4, nkc)):
                        nT += 1
                        tS[(qt, kc)] = nT
                    for kc in range(nkc):
                        if kc == nkc - 1:
                            nT += 1
                            tPV[qt] = nT
                        if kc + 4 < nkc:
                            nT += 1
                            tS[(qt, kc + 4)] = nT
                    for kc in range(nkc):
                        nS += 1
                        tE[(qt, kc)] = nS
                        if kc >= 4 * qt:
                            nV += 1
                            tM[(qt, kc)] = nV
                    nV += 1
                    tOev[qt] = nV
                nS += 1
                tRL = nS
                for qt in range(4):
                    nT += 1
                    tBC[qt] = nT
                    nV += 1
                    tOsc[qt] = nV
                prev_hS = rec.get(("hS", h - 2))
                prev_rl = rec.get("rl")
                prev_oev = rec.get("oev")
                prev_osc_live = [rec.get(("osc", 0)), rec.get(("osc", 1))]
                prev_expdone = dict(rec.get("expdone_map", {}))

                with nc.Block() as blk:
                    def sy_qk(sy, bb=bb, h=h, hp=hp, prev_hS=prev_hS):
                        if prev_hS:
                            sy.wait_ge(b.eT.h, prev_hS)
                        elif h < 2 and bb == 0:
                            sy.wait_ge(b.eT.h, b.eT.n)
                        elif h < 2:
                            sy.wait_ge(b.eT.h, rec[("bbproj", bb - 1)])
                        b.dS.dtick(sy.dma_start(
                            out=qh[hp][:],
                            in_=b.scr["s_qk"][64 * h:64 * (h + 1),
                                              S * bb:S * (bb + 1)]))
                        b.dS.dtick(sy.dma_start(
                            out=kh[hp][:],
                            in_=b.scr["s_qk"][D + 64 * h:D + 64 * (h + 1),
                                              S * bb:S * (bb + 1)]))
                        rec["qk"] = b.dS.n

                    def v_all(v, h=h, hp=hp, mcol=mcol, ch=ch, rb=rb,
                              prev_hS=prev_hS, prev_rl=prev_rl):
                        if prev_hS:
                            v.wait_ge(b.eT.h, prev_hS)
                        for qt in range(4):
                            for kc in range(4 * qt, 4 * qt + 4):
                                v.wait_ge(b.eS.h, tE[(qt, kc)])
                                off = 128 * kc - 512 * qt
                                par = kc % 4
                                ins = v.tensor_tensor(
                                    Pt[par][:], Pt[par][:],
                                    b.maskM[:, 384 - off:384 - off + 512],
                                    ALU.mult)
                                assert b.eV.tick(ins) == tM[(qt, kc)]
                            v.wait_ge(b.eT.h, tPV[qt])
                            if qt == 0 and prev_rl:
                                v.wait_ge(b.eS.h, prev_rl)
                            v.tensor_copy(
                                Osb[ch][rb:rb + 64, 512 * qt:512 * (qt + 1)],
                                pO[0:64, :])
                            ins = v.tensor_copy(
                                Lrow[64:65, 512 * qt:512 * (qt + 1)],
                                pO[64:65, :])
                            assert b.eV.tick(ins) == tOev[qt]
                        for qt in range(4):
                            v.wait_ge(b.eT.h, tBC[qt])
                            sl = Osb[ch][rb:rb + 64, 512 * qt:512 * (qt + 1)]
                            par2 = qt % 2
                            ins = v.tensor_tensor(sl, sl,
                                                  pB[par2][rb:rb + 64, :],
                                                  ALU.mult)
                            assert b.eV.tick(ins) == tOsc[qt]

                    def s_all(s, h=h):
                        for qt in range(4):
                            for kc in range(4 * qt + 4):
                                par = kc % 4
                                s.wait_ge(b.eT.h, tS[(qt, kc)])
                                ins = s.activation(Pt[par][:], pS[par][:],
                                                   AF.Exp, scale=0.125)
                                assert b.eS.tick(ins) == tE[(qt, kc)]
                        s.wait_ge(b.eV.h, tOev[3])
                        s.activation(RL[64:65, :], Lrow[64:65, :], AF.Ln)
                        ins = s.activation(Lrow[64:65, :], RL[64:65, :],
                                           AF.Exp, scale=-1.0)
                        assert b.eS.tick(ins) == tRL
                        rec["rl"] = tRL

                    def t_all(t, h=h, hp=hp, prev_oev=prev_oev,
                              prev_osc_live=prev_osc_live,
                              prev_expdone=prev_expdone):
                        t.wait_ge(b.dS.h, rec["qk"])
                        t.wait_ge(b.dS.h, rec["vb"])
                        last_exp_par = dict(prev_expdone)
                        last_oev = prev_oev

                        def emit_S(qt, kc):
                            par = kc % 4
                            prev = last_exp_par.get(par)
                            if prev:
                                t.wait_ge(b.eS.h, prev)
                            ins = t.matmul(
                                pS[par][:], kh[hp][:, 128 * kc:128 * (kc + 1)],
                                qh[hp][:, 512 * qt:512 * (qt + 1)],
                                start=True, stop=True)
                            assert b.eT.tick(ins) == tS[(qt, kc)]
                            last_exp_par[par] = tE[(qt, kc)]

                        for qt in range(4):
                            nkc = 4 * qt + 4
                            for kc in range(min(4, nkc)):
                                emit_S(qt, kc)
                            for kc in range(nkc):
                                par = kc % 4
                                diag = kc >= 4 * qt
                                if diag:
                                    t.wait_ge(b.eV.h, tM[(qt, kc)])
                                else:
                                    t.wait_ge(b.eS.h, tE[(qt, kc)])
                                if kc == 0 and last_oev:
                                    t.wait_ge(b.eV.h, last_oev)
                                ins = t.matmul(pO[0:65, :],
                                               vb[kc][:, 65 * h:65 * (h + 1)],
                                               Pt[par][:], start=(kc == 0),
                                               stop=(kc == nkc - 1))
                                if kc == nkc - 1:
                                    assert b.eT.tick(ins) == tPV[qt]
                                    last_oev = tOev[qt]
                                if kc + 4 < nkc:
                                    emit_S(qt, kc + 4)
                        for qt in range(4):
                            par2 = qt % 2
                            t.wait_ge(b.eS.h, tRL)
                            prevo = (prev_osc_live[par2] if qt < 2
                                     else tOsc[qt - 2])
                            if prevo:
                                t.wait_ge(b.eV.h, prevo)
                            ins = t.matmul(pB[par2][:], ones65[64:65, :],
                                           Lrow[64:65, 512 * qt:512 * (qt + 1)],
                                           start=True, stop=True)
                            assert b.eT.tick(ins) == tBC[qt]
                        rec["expdone_map"] = last_exp_par

                    blk.sync(sy_qk)
                    blk.scalar(s_all)
                    blk.vector(v_all)
                    blk.tensor(t_all)
                rec["oev"] = tOev[3]
                rec[("osc", 0)] = tOsc[2]
                rec[("osc", 1)] = tOsc[3]
                rec[("hS", h)] = b.eT.n

            with nc.Block() as blk:
                o_done = b.eV.n
                for qt in range(4):
                    for mc in range(8):
                        par = mc % 2

                        def t_pp(t, qt=qt, mc=mc, par=par, o_done=o_done):
                            if qt == 0 and mc == 0:
                                t.wait_ge(b.eV.h, o_done)
                                t.wait_ge(b.dG.h, b.dG.n)
                            prev = rec.get(("pgo", par))
                            if prev:
                                t.wait_ge(b.eV.h, prev)
                            for k in range(8):
                                ins = t.matmul(
                                    pS[par][:],
                                    Wp_sb[k][:, 128 * mc:128 * (mc + 1)],
                                    Osb[k][:, 512 * qt:512 * (qt + 1)],
                                    start=(k == 0), stop=(k == 7))
                            rec[("pp", qt, mc)] = b.eT.tick(ins)

                        def s_ev(s, qt=qt, mc=mc, par=par):
                            s.wait_ge(b.eT.h, rec[("pp", qt, mc)])
                            prev = rec.get(("a1", par))
                            if prev:
                                s.wait_ge(b.eT.h, prev)
                            ins = s.tensor_scalar(at[par][:], pS[par][:],
                                                  b.bproj_c[:, mc:mc + 1],
                                                  None, ALU.add)
                            rec[("pgo", par)] = b.eV.tick(ins)

                        def t_tr(t, qt=qt, mc=mc, par=par):
                            t.wait_ge(b.eV.h, rec[("pgo", par)])
                            prev = rec.get(("a1c", par))
                            if prev:
                                t.wait_ge(b.eV.h, prev)
                            for tc in range(4):
                                ins = t.transpose(
                                    pB[par][:, 128 * tc:128 * (tc + 1)],
                                    at[par][:, 128 * tc:128 * (tc + 1)],
                                    b.identf[:])
                            rec[("a1", par)] = b.eT.tick(ins)

                        def v_a1(v, qt=qt, mc=mc, par=par):
                            v.wait_ge(b.eT.h, rec[("a1", par)])
                            prevst = rec.get("a1st")
                            if prevst and mc < 2:
                                v.wait_ge(b.dS.h, prevst)
                            for tc in range(4):
                                col = 1024 * tc + 128 * mc
                                ins = v.tensor_copy(
                                    a1t[:, col:col + 128],
                                    pB[par][:, 128 * tc:128 * (tc + 1)])
                            rec[("a1c", par)] = b.eV.tick(ins)

                        blk.tensor(t_pp)
                        blk.vector(s_ev)
                        blk.tensor(t_tr)
                        blk.vector(v_a1)

                    def sy_a1(sy, qt=qt, bb=bb):
                        sy.wait_ge(b.eV.h, b.eV.n)
                        for tc in range(4):
                            r0 = S * bb + 512 * qt + 128 * tc
                            b.dS.dtick(sy.dma_start(
                                out=b.scr["s_attn1"][r0:r0 + 128, :],
                                in_=a1t[:, 1024 * tc:1024 * (tc + 1)]))
                        rec["a1st"] = b.dS.n

                    blk.sync(sy_a1)
            rec[("bbproj", bb)] = b.eT.n
    b.barrier()


def phase_p1c(b):
    nc = b.nc
    rec = {}
    with ExitStack() as st:
        sbn = lambda n, s, d=F32: st.enter_context(nc.sbuf_tensor(n, s, d))
        xnbt = sbn("c_xnb", [128, 4096], BF16)
        xnfm = [sbn(f"c_fm{k}", [128, 512], BF16) for k in range(8)]
        Wr_sb = [sbn(f"c_wr{k}", [128, 8], BF16) for k in range(8)]
        r_m = sbn("c_m", [128, 1])
        r_mneg = sbn("c_mneg", [128, 1])
        r_e8 = sbn("c_e8", [128, 8])
        r_s8 = sbn("c_s8", [128, 1])
        r_eq = sbn("c_eq", [128, 8])
        r_mx = sbn("c_mx", [128, 1])
        X1 = sbn("c_X1", [8, T])
        X2 = sbn("c_X2", [8, T])
        X3 = sbn("c_X3", [8, T])
        R1 = sbn("c_R1", [1, T])
        R2 = sbn("c_R2", [1, T])
        R3 = sbn("c_R3", [1, T])
        R4 = sbn("c_R4", [1, T])
        idxT = sbn("c_idxT", [32, 128])
        backT = sbn("c_backT", [32, 128])
        pr = st.enter_context(nc.psum_tensor("c_pr", [128, 512], F32))
        pT = st.enter_context(nc.psum_tensor("c_pT", [128, 512], F32))
        pst = [st.enter_context(nc.psum_tensor(f"c_ps{j}", [128, 512], BF16))
               for j in range(2)]

        with nc.Block() as blk:
            def g_pre(g):
                b.bwait(g)
                b.eG.tick(g.iota(R1[:], [[1, T]], channel_multiplier=0,
                                 allow_small_or_imprecise_dtypes=True))
                for k in range(8):
                    b.dG.dtick(g.dma_start(
                        out=Wr_sb[k][:],
                        in_=b.in_ap["L1_ffn_Wr"][128 * k:128 * (k + 1), :]))

            def v_pre(v):
                v.wait_ge(b.eG.h, b.eG.n)
                v.memset(X3[:], 0.0)
                rec["tbig"] = b.eV.tick(v.tensor_scalar(
                    R1[:], R1[:], 1.0, BIG, ALU.mult, ALU.add))

            blk.gpsimd(g_pre)
            blk.vector(v_pre)
        w_ld = b.dG.n

        for t_i in range(NT):
            with nc.Block() as blk:
                def sy_ld(sy, t_i=t_i):
                    sy.wait_ge(b.eT.h, b.eT.n)
                    for ch in range(4):
                        r0 = TT * t_i + 128 * ch
                        b.dS.dtick(sy.dma_start(
                            out=xnbt[:, 1024 * ch:1024 * (ch + 1)],
                            in_=b.scr["s_xn1b"][r0:r0 + 128, :]))
                    rec["ld"] = b.dS.n
                blk.sync(sy_ld)

                last_copy = {}
                for kc in range(8):
                    par = kc % 2

                    def t_grp(t, kc=kc, par=par):
                        t.wait_ge(b.dS.h, rec["ld"])
                        prev = last_copy.get(par)
                        if prev:
                            t.wait_ge(b.eV.h, prev)
                        for ch in range(4):
                            ins = t.transpose(
                                pst[par][:, 128 * ch:128 * (ch + 1)],
                                xnbt[:, 1024 * ch + 128 * kc:
                                     1024 * ch + 128 * (kc + 1)],
                                b.identb[:])
                        rec[("t", kc)] = b.eT.tick(ins)

                    def v_grp(v, kc=kc, par=par):
                        v.wait_ge(b.eT.h, rec[("t", kc)])
                        ins = v.tensor_copy(xnfm[kc][:], pst[par][:])
                        last_copy[par] = b.eV.tick(ins)
                        rec[("fm", kc)] = last_copy[par]

                    blk.tensor(t_grp)
                    blk.vector(v_grp)

                for tc in range(4):
                    col = t_i * 4 + tc

                    def t_r(t, tc=tc, t_i=t_i):
                        if tc == 0 and t_i == 0:
                            t.wait_ge(b.dG.h, w_ld)
                        for k in range(8):
                            t.wait_ge(b.eV.h, rec[("fm", k)])
                        prev = rec.get("pr_free")
                        if prev:
                            t.wait_ge(b.eV.h, prev)
                        for k in range(8):
                            ins = t.matmul(pr[:, 0:8],
                                           xnfm[k][:, 128 * tc:128 * (tc + 1)],
                                           Wr_sb[k][:], start=(k == 0),
                                           stop=(k == 7))
                        rec[("rmm", col)] = b.eT.tick(ins)

                    def v_r1(v, col=col):
                        v.wait_ge(b.eT.h, rec[("rmm", col)])
                        b.gap(v, b.eV, v.tensor_reduce(r_m[:], pr[:, 0:8],
                                                       AX.X, ALU.max))
                        rec[("rm", col)] = b.eV.tick(v.tensor_scalar(
                            r_mneg[:], r_m[:], -1.0, None, ALU.mult))
                        v.wait_ge(b.eV.h, rec[("rm", col)])

                    def s_r(s, col=col):
                        s.wait_ge(b.eV.h, rec[("rm", col)])
                        rec[("re", col)] = b.eS.tick(s.activation(
                            r_e8[:], pr[:, 0:8], AF.Exp,
                            bias=r_mneg[:, 0:1]))

                    def v_r2(v, col=col):
                        v.wait_ge(b.eS.h, rec[("re", col)])
                        b.gap(v, b.eV, v.tensor_reduce(r_s8[:], r_e8[:], AX.X,
                                                       ALU.add))
                        b.gap(v, b.eV, v.reciprocal(
                            b.pmax_tm[:, col:col + 1], r_s8[:]))
                        b.gap(v, b.eV, v.tensor_tensor(
                            r_eq[:], pr[:, 0:8],
                            r_m[:, 0:1].broadcast_to([128, 8]), ALU.is_equal))
                        rec["pr_free"] = b.eV.n
                        b.gap(v, b.eV, v.tensor_tensor(r_eq[:], r_eq[:],
                                                       b.a9[:], ALU.mult))
                        b.gap(v, b.eV, v.tensor_reduce(r_mx[:], r_eq[:], AX.X,
                                                       ALU.max))
                        b.gap(v, b.eV, v.tensor_scalar(
                            b.idx_tm[:, col:col + 1], r_mx[:], -1.0, 9.0,
                            ALU.mult, ALU.add))

                    blk.tensor(t_r)
                    blk.vector(v_r1)
                    blk.scalar(s_r)
                    blk.vector(v_r2)

        with nc.Block() as blk:
            def t_tp(t):
                t.wait_ge(b.eV.h, b.eV.n)
                rec["tp"] = b.eT.tick(t.transpose(pT[0:32, 0:128], b.idx_tm[:],
                                                  b.identf[:]))

            def v_tp(v):
                v.wait_ge(b.eT.h, rec["tp"])
                rec["idxT"] = b.eV.tick(v.tensor_copy(idxT[:], pT[0:32, 0:128]))

            def sy_rt(sy):
                sy.wait_ge(b.eV.h, rec["idxT"])
                b.dS.dtick(sy.dma_start(
                    out=b.scr["s_rt"].rearrange("(a c) -> a c", c=128),
                    in_=idxT[:]))
                rec["rt_w"] = b.dS.n

            def sy_rt2(sy):
                sy.wait_ge(b.dS.h, rec["rt_w"])
                b.dS.dtick(sy.dma_start(out=R2[:],
                                        in_=b.scr["s_rt"].unsqueeze(0)))
                rec["rt_r"] = b.dS.n

            blk.tensor(t_tp)
            blk.vector(v_tp)
            blk.sync(sy_rt)
            blk.sync(sy_rt2)

            for j in range(8):
                def t_bc(t, j=j):
                    t.wait_ge(b.dS.h, rec["rt_r"])
                    prev = rec.get(("bcc", (j - 1) % 2))
                    if prev:
                        t.wait_ge(b.eV.h, prev)
                    rec[("bc", j)] = b.eT.tick(t.matmul(
                        pT[0:8, 0:512], b.ones1f8[:],
                        R2[:, 512 * j:512 * (j + 1)], start=True, stop=True))

                def v_bc(v, j=j):
                    v.wait_ge(b.eT.h, rec[("bc", j)])
                    ins = v.tensor_copy(X1[:, 512 * j:512 * (j + 1)],
                                        pT[0:8, 0:512])
                    rec[("bcc", j % 2)] = b.eV.tick(ins)

                blk.tensor(t_bc)
                blk.vector(v_bc)

            def v_route(v):
                v.wait_ge(b.eV.h, b.eV.n)
                b.gap(v, b.eV, v.tensor_scalar(X2[:], X1[:],
                                               b.iota8v[:, 0:1], None,
                                               ALU.is_equal))
                b.gap(v, b.eV, v.tensor_tensor_scan(
                    X1[:], X2[:], X3[:], 0.0, ALU.add, ALU.add))
                b.gap(v, b.eV, v.tensor_scalar(X1[:], X1[:], 1.0, None,
                                               ALU.subtract))
                b.gap(v, b.eV, v.tensor_scalar(X3[:], X1[:],
                                               float(LCAP), None, ALU.is_lt))
                b.gap(v, b.eV, v.tensor_tensor(X3[:], X2[:], X3[:], ALU.mult))
                b.gap(v, b.eV, v.tensor_scalar(X2[:], X1[:],
                                               b.iota768e[:, 0:1], None,
                                               ALU.add))
                rec["route"] = b.eV.tick(v.tensor_tensor(X2[:], X2[:], X3[:],
                                                         ALU.mult))
                v.wait_ge(b.eV.h, rec["route"])

            blk.vector(v_route)

            for j in range(8):
                def t_cl(t, j=j):
                    if j == 0:
                        t.wait_ge(b.eV.h, rec["route"])
                    prev = rec.get(("clc", (j - 1) % 2))
                    if prev:
                        t.wait_ge(b.eV.h, prev)
                    t.matmul(pT[0:1, 0:512], b.ones8f[:],
                             X2[:, 512 * j:512 * (j + 1)], start=True,
                             stop=True)
                    rec[("cl", j)] = b.eT.tick(t.matmul(
                        pr[0:1, 0:512], b.ones8f[:],
                        X3[:, 512 * j:512 * (j + 1)], start=True, stop=True))

                def v_cl(v, j=j):
                    v.wait_ge(b.eT.h, rec[("cl", j)])
                    v.tensor_copy(R2[:, 512 * j:512 * (j + 1)],
                                  pT[0:1, 0:512])
                    ins = v.tensor_copy(R3[:, 512 * j:512 * (j + 1)],
                                        pr[0:1, 0:512])
                    rec[("clc", j % 2)] = b.eV.tick(ins)

                blk.tensor(t_cl)
                blk.vector(v_cl)

            def v_dest(v):
                v.wait_ge(b.eV.h, b.eV.n)
                b.gap(v, b.eV, v.tensor_scalar(R4[:], R3[:], -1.0, 1.0,
                                               ALU.mult, ALU.add))
                b.gap(v, b.eV, v.tensor_tensor(R4[:], R4[:], R1[:], ALU.mult))
                rec["drow"] = b.eV.tick(v.tensor_tensor(R1[:], R2[:], R4[:],
                                                        ALU.add))
                v.wait_ge(b.eV.h, rec["drow"])

            blk.vector(v_dest)

            def sy_dr(sy):
                sy.wait_ge(b.eV.h, rec["drow"])
                b.dS.dtick(sy.dma_start(out=b.scr["s_rt2"].unsqueeze(0),
                                        in_=R1[:]))
                b.dS.dtick(sy.dma_start(out=b.scr["s_rt3"].unsqueeze(0),
                                        in_=R3[:]))
                rec["dr_w"] = b.dS.n

            def sy_dr2(sy):
                sy.wait_ge(b.dS.h, rec["dr_w"])
                b.dS.dtick(sy.dma_start(
                    out=backT[:],
                    in_=b.scr["s_rt2"].rearrange("(a c) -> a c", c=128)))
                rec["dr_r"] = b.dS.n

            def t_dt(t):
                t.wait_ge(b.dS.h, rec["dr_r"])
                rec["dt"] = b.eT.tick(t.transpose(pT[0:128, 0:32], backT[:],
                                                  b.identf[0:32, 0:32]))

            def v_dt(v):
                v.wait_ge(b.eT.h, rec["dt"])
                rec["dest"] = b.eV.tick(v.tensor_copy(b.dest_tm[:],
                                                      pT[0:128, 0:32]))

            def sy_kr(sy):
                sy.wait_ge(b.eV.h, rec["dest"])
                b.dS.dtick(sy.dma_start(
                    out=backT[:],
                    in_=b.scr["s_rt3"].rearrange("(a c) -> a c", c=128)))
                rec["kr_r"] = b.dS.n

            def t_kt(t):
                t.wait_ge(b.dS.h, rec["kr_r"])
                rec["kt"] = b.eT.tick(t.transpose(pT[0:128, 0:32], backT[:],
                                                  b.identf[0:32, 0:32]))

            def v_kt(v):
                v.wait_ge(b.eT.h, rec["kt"])
                b.gap(v, b.eV, v.tensor_copy(b.kept_tm[:], pT[0:128, 0:32]))
                b.gap(v, b.eV, v.tensor_scalar(b.factor_tm[:], b.pmax_tm[:],
                                               1.0, None, ALU.subtract))
                b.gap(v, b.eV, v.tensor_tensor(b.factor_tm[:], b.kept_tm[:],
                                               b.factor_tm[:], ALU.mult))
                rec["factor"] = b.eV.tick(v.tensor_scalar(
                    b.factor_tm[:], b.factor_tm[:], 1.0, None, ALU.add))

            blk.sync(sy_dr)
            blk.sync(sy_dr2)
            blk.tensor(t_dt)
            blk.vector(v_dt)
            blk.sync(sy_kr)
            blk.tensor(t_kt)
            blk.vector(v_kt)

        disp_t = [sbn(f"c_dsp{j}", [128, D], BF16) for j in range(2)]
        with nc.Block() as blk:
            for c in range(32):
                par = c % 2

                def sy_dl(sy, c=c, par=par):
                    prev = rec.get(("scat", par))
                    if prev:
                        sy.wait_ge(b.dG.h, prev)
                    b.dS.dtick(sy.dma_start(
                        out=disp_t[par][:],
                        in_=b.scr["s_xn1b"][128 * c:128 * (c + 1), :]))
                    rec[("dl", c)] = b.dS.n

                def g_sc(g, c=c, par=par):
                    g.wait_ge(b.dS.h, rec[("dl", c)])
                    if c == 0:
                        g.wait_ge(b.eV.h, rec["dest"])
                    if "bc_reg" not in rec:
                        rec["bc_reg"] = g.to_reg(NEXP * LCAP - 1)
                    ins = g.indirect_dma_start(
                        out=b.scr["s_D"],
                        out_offset=bass.IndirectOffsetOnAxis(
                            ap=b.dest_tm[:, c:c + 1], axis=0),
                        in_=disp_t[par][:],
                        in_offset=None,
                        bounds_check=rec["bc_reg"],
                        oob_is_err=False)
                    rec[("scat", par)] = b.dG.dtick(ins)

                blk.sync(sy_dl)
                blk.gpsimd(g_sc)
        scat_done = b.dG.n

    with ExitStack() as st:
        sbn = lambda n, s, d=F32: st.enter_context(nc.sbuf_tensor(n, s, d))
        W1_sb = [sbn(f"c_w1_{k}", [128, 2048], BF16) for k in range(8)]
        W2_sb = [sbn(f"c_w2_{k}", [128, D], BF16) for k in range(32)]
        Dtm = [sbn(f"c_dtm{k}", [128, D], BF16) for k in range(6)]
        Dfm = [sbn(f"c_dfm{k}", [128, LCAP], BF16) for k in range(8)]
        Hfm = [sbn(f"c_hf{k}", [128, LCAP], BF16) for k in range(32)]
        eo = [sbn(f"c_eo{j}", [128, D], F32) for j in range(2)]
        pst = [st.enter_context(nc.psum_tensor(f"c_eps{j}", [128, 512], BF16))
               for j in range(2)]
        pm1 = [st.enter_context(nc.psum_tensor(f"c_pm1{j}", [128, 512], F32))
               for j in range(2)]
        pm2 = [st.enter_context(nc.psum_tensor(f"c_pm2{j}", [128, 512], F32))
               for j in range(2)]

        for e in range(NEXP):
            with nc.Block() as blk:
                def g_w1(g, e=e):
                    if e == 0:
                        g.wait_ge(b.dG.h, scat_done)
                        g.wait_ge(b.eV.h, b.eV.n)
                    g.wait_ge(b.eT.h, b.eT.n)
                    for k in range(8):
                        b.dG.dtick(g.dma_start(
                            out=W1_sb[k][:],
                            in_=b.in_ap["L1_ffn_W1"][e, 128 * k:128 * (k + 1),
                                                     0:2048]))
                    for k in range(32):
                        b.dG.dtick(g.dma_start(
                            out=W2_sb[k][:],
                            in_=b.in_ap["L1_ffn_W2"][e, 128 * k:128 * (k + 1), :]))
                    rec["w12"] = b.dG.n

                def sy_d(sy, e=e):
                    sy.wait_ge(b.dG.h, scat_done)
                    sy.wait_ge(b.eT.h, b.eT.n)
                    for k in range(6):
                        r0 = LCAP * e + 128 * k
                        b.dS.dtick(sy.dma_start(out=Dtm[k][:],
                                                in_=b.scr["s_D"][r0:r0 + 128, :]))
                    rec["dld"] = b.dS.n

                blk.gpsimd(g_w1)
                blk.sync(sy_d)

                last_copy = {}
                for k in range(6):
                    for kc in range(8):
                        gi = k * 8 + kc
                        par = gi % 2

                        def t_dt2(t, k=k, kc=kc, par=par, gi=gi):
                            t.wait_ge(b.dS.h, rec["dld"])
                            prev = last_copy.get(par)
                            if prev:
                                t.wait_ge(b.eV.h, prev)
                            rec[("dt2", gi)] = b.eT.tick(t.transpose(
                                pst[par][:, 0:128],
                                Dtm[k][:, 128 * kc:128 * (kc + 1)],
                                b.identb[:]))

                        def v_dt2(v, k=k, kc=kc, par=par, gi=gi):
                            v.wait_ge(b.eT.h, rec[("dt2", gi)])
                            ins = v.tensor_copy(
                                Dfm[kc][:, 128 * k:128 * (k + 1)],
                                pst[par][:, 0:128])
                            last_copy[par] = b.eV.tick(ins)

                        blk.tensor(t_dt2)
                        blk.vector(v_dt2)
                dfm_done = b.eV.n

                for fhalf in range(2):
                    if fhalf == 1:
                        def g_w1b(g, e=e):
                            g.wait_ge(b.eT.h, rec[("m1", 15)])
                            for k in range(8):
                                b.dG.dtick(g.dma_start(
                                    out=W1_sb[k][:],
                                    in_=b.in_ap["L1_ffn_W1"][
                                        e, 128 * k:128 * (k + 1), 2048:4096]))
                            rec["w12b"] = b.dG.n
                        blk.gpsimd(g_w1b)

                    for fl in range(16):
                        fc = fhalf * 16 + fl
                        par = fc % 2

                        def t_m1(t, fc=fc, fl=fl, par=par, fhalf=fhalf,
                                 dfm_done=dfm_done):
                            if fc == 0:
                                t.wait_ge(b.eV.h, dfm_done)
                                t.wait_ge(b.dG.h, rec["w12"])
                            if fhalf == 1 and fl == 0:
                                t.wait_ge(b.dG.h, rec["w12b"])
                            prevs = rec.get(("h_ev", par))
                            if prevs:
                                t.wait_ge(b.eV.h, prevs[1])
                            for k in range(8):
                                t.matmul(pm1[par][:, 0:512],
                                         W1_sb[k][:, 128 * fl:128 * (fl + 1)],
                                         Dfm[k][:, 0:512], start=(k == 0),
                                         stop=(k == 7))
                            for k in range(8):
                                ins = t.matmul(
                                    pm2[par][:, 0:256],
                                    W1_sb[k][:, 128 * fl:128 * (fl + 1)],
                                    Dfm[k][:, 512:LCAP], start=(k == 0),
                                    stop=(k == 7))
                            rec[("m1", fc)] = b.eT.tick(ins)

                        def v_rel(v, fc=fc, par=par):
                            v.wait_ge(b.eT.h, rec[("m1", fc)])
                            v.tensor_scalar(Hfm[fc][:, 0:512],
                                            pm1[par][:, 0:512], 0.0, None,
                                            ALU.max)
                            ins = v.tensor_scalar(Hfm[fc][:, 512:LCAP],
                                                  pm2[par][:, 0:256], 0.0,
                                                  None, ALU.max)
                            rec[("h_ev", par)] = (b.eV.n, b.eV.tick(ins))

                        blk.tensor(t_m1)
                        blk.vector(v_rel)

                for rbp in range(3):
                    for sub in range(2):
                        rb = 2 * rbp + sub
                        for half in range(2):
                            pp = (pm1 if sub == 0 else pm2)[half]

                            def t_m2(t, rb=rb, sub=sub, half=half, pp=pp,
                                     rbp=rbp):
                                if rbp == 0 and sub == 0 and half == 0:
                                    t.wait_ge(b.eV.h, b.eV.n)
                                prev = rec.get(("eoc", sub, half))
                                if prev:
                                    t.wait_ge(b.eV.h, prev)
                                for fc in range(32):
                                    ins = t.matmul(
                                        pp[:, 0:512],
                                        Hfm[fc][:, 128 * rb:128 * (rb + 1)],
                                        W2_sb[fc][:, 512 * half:
                                                  512 * (half + 1)],
                                        start=(fc == 0), stop=(fc == 31))
                                rec[("m2", rb, half)] = b.eT.tick(ins)

                            def v_eo(v, rb=rb, sub=sub, half=half, pp=pp):
                                v.wait_ge(b.eT.h, rec[("m2", rb, half)])
                                prev = rec.get(("eost", sub))
                                if prev:
                                    v.wait_ge(b.dS.h, prev)
                                ins = v.tensor_copy(
                                    eo[sub][:, 512 * half:512 * (half + 1)],
                                    pp[:, 0:512])
                                rec[("eoc", sub, half)] = b.eV.tick(ins)

                            blk.tensor(t_m2)
                            blk.vector(v_eo)

                        def sy_eo(sy, rb=rb, sub=sub, e=e):
                            sy.wait_ge(b.eV.h, rec[("eoc", sub, 0)])
                            sy.wait_ge(b.eV.h, rec[("eoc", sub, 1)])
                            r0 = LCAP * e + 128 * rb
                            b.dS.dtick(sy.dma_start(
                                out=b.scr["s_eout"][r0:r0 + 128, :],
                                in_=eo[sub][:]))
                            rec[("eost", sub)] = b.dS.n

                        blk.sync(sy_eo)
        eout_done_S = b.dS.n

    with ExitStack() as st:
        sbn = lambda n, s, d=F32: st.enter_context(nc.sbuf_tensor(n, s, d))
        combt = [sbn(f"c_cb{j}", [128, D], F32) for j in range(4)]
        x1c = [sbn(f"c_x1c{j}", [128, D], F32) for j in range(4)]
        outt = [sbn(f"c_ot{j}", [128, D], F32) for j in range(4)]
        with nc.Block() as blk:
            for c in range(32):
                par = c % 4

                def sy_cl(sy, c=c, par=par):
                    if c == 0:
                        sy.wait_ge(b.dS.h, eout_done_S)
                        sy.wait_ge(b.eV.h, b.eV.n)
                    prev = rec.get(("outst", par))
                    if prev:
                        sy.wait_ge(b.dS.h, prev)
                    prevg = rec.get(("gat", par))
                    if prevg:
                        sy.wait_ge(b.dG.h, prevg)
                    b.dS.dtick(sy.dma_start(
                        out=combt[par][:],
                        in_=b.scr["s_xn1"][128 * c:128 * (c + 1), :]))
                    rec[("cl", c)] = b.dS.n

                def g_cl(g, c=c, par=par):
                    prev = rec.get(("outv", par))
                    if prev:
                        g.wait_ge(b.eV.h, prev)
                    if c == 0:
                        g.wait_ge(b.dS.h, eout_done_S)
                    n0 = b.dG.dtick(g.dma_start(
                        out=x1c[par][:],
                        in_=b.scr["s_x1"][128 * c:128 * (c + 1), :]))
                    g.wait_ge(b.dG.h, n0)
                    b.dG.dtick(g.dma_start(
                        out=x1c[par][:],
                        in_=b.scr["s_attn1"][128 * c:128 * (c + 1), :],
                        accum_op=ALU.add))
                    g.wait_ge(b.dS.h, rec[("cl", c)])
                    if "bc_reg2" not in rec:
                        rec["bc_reg2"] = g.to_reg(NEXP * LCAP - 1)
                    ins = g.indirect_dma_start(
                        out=combt[par][:],
                        out_offset=None,
                        in_=b.scr["s_eout"],
                        in_offset=bass.IndirectOffsetOnAxis(
                            ap=b.dest_tm[:, c:c + 1], axis=0),
                        bounds_check=rec["bc_reg2"],
                        oob_is_err=False)
                    rec[("gat", par)] = b.dG.dtick(ins)

                def v_out(v, c=c, par=par):
                    v.wait_ge(b.dG.h, rec[("gat", par)])
                    v.tensor_scalar(outt[par][:], combt[par][:],
                                    b.factor_tm[:, c:c + 1], None, ALU.mult)
                    ins = v.tensor_tensor(outt[par][:], outt[par][:],
                                          x1c[par][:], ALU.add)
                    rec[("outv", par)] = b.eV.tick(ins)

                def sy_out(sy, c=c, par=par):
                    sy.wait_ge(b.eV.h, rec[("outv", par)])
                    b.dS.dtick(sy.dma_start(
                        out=b.out_ap[128 * c:128 * (c + 1), :],
                        in_=outt[par][:]))
                    rec[("outst", par)] = b.dS.n

                blk.sync(sy_cl)
                blk.gpsimd(g_cl)
                blk.vector(v_out)
                blk.sync(sy_out)
    b.barrier()


def kernel(**inputs):
    x = np.ascontiguousarray(inputs["L0_attn_x"], dtype=np.float32)
    shards = [np.ascontiguousarray(x[2 * c:2 * c + 2].reshape(T, D))
              for c in range(NCORE)]
    nc = build()
    in_maps = []
    for c in range(NCORE):
        m = {"x": shards[c]}
        for nm in W_NAMES:
            m[nm] = np.ascontiguousarray(inputs[nm], dtype=np.float32)
        in_maps.append(m)
    res = run_bass_kernel_spmd(nc, in_maps, core_ids=list(range(NCORE)))
    out = np.concatenate([r["out"].reshape(2, S, D) for r in res.results], axis=0)
    return out


# version 09 (final)
# speedup vs baseline: 1.0653x; 1.0653x over previous
import numpy as np
import ml_dtypes
from contextlib import ExitStack
import concourse.bass as bass
import concourse.mybir as mybir
from concourse.bass_utils import run_bass_kernel_spmd

F32 = mybir.dt.float32
BF16 = mybir.dt.bfloat16
U32 = mybir.dt.uint32
U8 = mybir.dt.uint8
AF = mybir.ActivationFunctionType
ALU = mybir.AluOpType
AX = mybir.AxisListType

NCORE = 8
T = 4096
D = 1024
S = 2048
NT = 8
TT = 512
I_FFN = 2816
NI = 22
DFF = 4096
NEXP = 8
LCAP = 768
NH = 16
DH = 64
BIG = 1.0e6
C05, C1_, C2_ = 6291456.0, 12582912.0, 25165824.0
INV6 = 1.0 / 6.0

W_NAMES = [
    "L0_attn_Wq", "L0_attn_bq", "L0_attn_Wk", "L0_attn_bk", "L0_attn_Wv",
    "L0_attn_bv", "L0_attn_Wo", "L0_attn_bo", "L0_ffn_w_gate", "L0_ffn_w_up",
    "L0_ffn_w_down", "L1_attn_W_attn", "L1_attn_b_attn", "L1_attn_W_proj",
    "L1_attn_b_proj", "L1_ffn_Wr", "L1_ffn_W1", "L1_ffn_W2",
]
W_SHAPES = {
    "L0_attn_Wq": [D, D], "L0_attn_bq": [D], "L0_attn_Wk": [D, D],
    "L0_attn_bk": [D], "L0_attn_Wv": [D, D], "L0_attn_bv": [D],
    "L0_attn_Wo": [D, D], "L0_attn_bo": [D],
    "L0_ffn_w_gate": [I_FFN, D], "L0_ffn_w_up": [I_FFN, D],
    "L0_ffn_w_down": [D, I_FFN],
    "L1_attn_W_attn": [D, 3 * D], "L1_attn_b_attn": [3 * D],
    "L1_attn_W_proj": [D, D], "L1_attn_b_proj": [D],
    "L1_ffn_Wr": [D, NEXP], "L1_ffn_W1": [NEXP, D, DFF],
    "L1_ffn_W2": [NEXP, DFF, D],
}


class Sem:
    def __init__(self, nc, ctx, name):
        self.h = ctx.enter_context(nc.semaphore(name))
        self.n = 0

    def tick(self, instr, k=1):
        instr.then_inc(self.h, k)
        self.n += k
        return self.n

    def dtick(self, instr):
        return self.tick(instr, 16)


class B:

    def __init__(self, nc, ctx, dbg=()):
        self.nc = nc
        self.ctx = ctx
        self.dbg = set(dbg)
        self.dbg_list = []
        self.in_ap = {}
        self.scr = {}
        self.in_ap["x"] = nc.dram_tensor("x", [T, D], F32, kind="ExternalInput").ap()
        for nm in W_NAMES:
            self.in_ap[nm] = nc.dram_tensor(nm, W_SHAPES[nm], F32,
                                            kind="ExternalInput").ap()
        self.out_ap = nc.dram_tensor("out", [T, D], F32, kind="ExternalOutput").ap()
        def scrd(name, shape, dt):
            kind = "ExternalOutput" if name in self.dbg else "Internal"
            self.scr[name] = nc.dram_tensor(name, shape, dt, kind=kind).ap()
        scrd("s_q0", [D, T], F32)
        scrd("s_xq", [D, T], BF16)
        scrd("s_wgq", [D, I_FFN], BF16)
        scrd("s_wuq", [D, I_FFN], BF16)
        scrd("s_wdq", [I_FFN, D], BF16)
        scrd("s_hq", [I_FFN, T], BF16)
        scrd("s_ffn0", [D, T], F32)
        scrd("s_x1", [T, D], F32)
        scrd("s_xn1", [T, D], F32)
        scrd("s_xn1b", [T, D], BF16)
        scrd("s_qk", [2 * D, T], BF16)
        scrd("s_v", [T, NH * 65], BF16)
        scrd("s_attn1", [T, D], F32)
        scrd("s_negm", [2, 16], F32)
        scrd("s_D", [NEXP * LCAP, D], BF16)
        scrd("s_eout", [NEXP * LCAP, D], F32)
        scrd("s_rt", [T], F32)
        scrd("s_rt2", [T], F32)
        scrd("s_rt3", [T], F32)

        self.dS = Sem(nc, ctx, "dS")
        self.dG = Sem(nc, ctx, "dG")
        self.eT = Sem(nc, ctx, "eT")
        self.eV = Sem(nc, ctx, "eV")
        self.eS = Sem(nc, ctx, "eS")
        self.eG = Sem(nc, ctx, "eG")
        self.allsems = [self.dS, self.dG, self.eT, self.eV, self.eS, self.eG]
        self.bar = []

        def sb(name, shape, dt=F32):
            return ctx.enter_context(nc.sbuf_tensor(name, shape, dt))
        self.sb = sb

        self.iotaP = sb("iotaP", [128, 1])
        self.iota8 = sb("iota8", [128, 8])
        self.a9 = sb("a9", [128, 8])
        self.iota8v = sb("iota8v", [8, 1])
        self.iota768e = sb("iota768e", [8, 1])
        self.identf = sb("identf", [128, 128])
        self.identb = sb("identb", [128, 128], BF16)
        self.maskM = sb("maskM", [128, 896], BF16)
        self.h2 = sb("h2", [128, 2], BF16)
        self.ones1f = sb("ones1f", [1, 128])
        self.ones8f = sb("ones8f", [8, 1])
        self.ones1f8 = sb("ones1f8", [1, 8])
        self.onetokb = sb("onetokb", [1, 128], BF16)
        self.bq_c = sb("bq_c", [128, 8])
        self.bk_c = sb("bk_c", [128, 8])
        self.bv_c = sb("bv_c", [128, 8])
        self.bo_c = sb("bo_c", [128, 8])
        self.battn_c = sb("battn_c", [128, 24])
        self.bproj_c = sb("bproj_c", [128, 8])
        self.kvacc = sb("kvacc", [128, 16])
        self.ksacc = sb("ksacc", [128, 16])
        self.qrm = sb("qrm", [128, 16])
        self.krm = sb("krm", [128, 16])
        self.idx_tm = sb("idx_tm", [128, 32])
        self.pmax_tm = sb("pmax_tm", [128, 32])
        self.kept_tm = sb("kept_tm", [128, 32])
        self.factor_tm = sb("factor_tm", [128, 32])
        self.dest_tm = sb("dest_tm", [128, 32], U32)
        self.negm32 = sb("negm32", [128, 32])

    def barrier(self):
        self.bar = [(s, s.n) for s in self.allsems]

    def bwait(self, eng):
        for sem, val in self.bar:
            if val > 0:
                eng.wait_ge(sem.h, val)

    def gap(self, eng, sem, instr):
        n = sem.tick(instr)
        eng.wait_ge(sem.h, n)

    def dbg_store(self, name, ap, waits):
        if name in self.dbg:
            shape = list(ap.shape)
            dt = ap.dtype
            d = self.nc.dram_tensor("dbg_" + name, shape, dt,
                                    kind="ExternalOutput").ap()
            self.dbg_list.append((d, ap, list(waits)))

    def flush_dbg_and_finish(self):
        nc = self.nc
        with nc.Block() as blk:
            def g_fn(g):
                self.bwait(g)
                for d, ap, waits in self.dbg_list:
                    for sem, val in waits:
                        g.wait_ge(sem.h, val)
                    self.dG.dtick(g.dma_start(out=d, in_=ap))
                g.wait_ge(self.dS.h, self.dS.n)
                g.wait_ge(self.dG.h, self.dG.n)
            blk.gpsimd(g_fn)

    def quant_job(self, x_ap, out_ap, ncols):
        b = self

        class QJ:
            def v1(self, v):
                ng = ncols // 16
                x3 = x_ap.rearrange("p (g k) -> p g k", k=16)
                b.gap(v, b.eV, v.tensor_reduce(b.qsc[:, 0:ng].unsqueeze(2), x3,
                                               AX.X, ALU.max,
                                               apply_absolute_value=True))
                b.gap(v, b.eV, v.tensor_scalar(b.qsc[:, 0:ng], b.qsc[:, 0:ng],
                                               INV6, 1e-12, ALU.mult, ALU.max))
                b.gap(v, b.eV, v.reciprocal(b.qrs[:, 0:ng], b.qsc[:, 0:ng]))
                rb = b.qrs[:, 0:ng].unsqueeze(2).broadcast_to([128, ng, 16])
                v.tensor_tensor(b.qu[:, 0:ncols].rearrange(
                    "p (g k) -> p g k", k=16), x3, rb, ALU.mult)
                u = b.qu[:, 0:ncols]
                v.tensor_scalar(b.qr05[:, 0:ncols], u, C05, -C05, ALU.add,
                                ALU.add)
                v.tensor_scalar(b.qq[:, 0:ncols], u, C1_, -C1_, ALU.add,
                                ALU.add)
                v.tensor_scalar(b.qr2[:, 0:ncols], u, C2_, -C2_, ALU.add,
                                ALU.add)
                v.scalar_tensor_tensor(b.qu2[:, 0:ncols], u, 0.0, u,
                                       ALU.bypass, ALU.mult)
                v.tensor_scalar(b.qm1[:, 0:ncols], b.qu2[:, 0:ncols],
                                4.515625, None, ALU.is_lt)
                v.tensor_scalar(b.qm3[:, 0:ncols], b.qu2[:, 0:ncols], 16.0,
                                None, ALU.is_ge)
                v.copy_predicated(b.qq[:, 0:ncols], b.qm1[:, 0:ncols],
                                  b.qr05[:, 0:ncols])
                v.copy_predicated(b.qq[:, 0:ncols], b.qm3[:, 0:ncols],
                                  b.qr2[:, 0:ncols])
                ng = ncols // 16
                scb = b.qsc[:, 0:ng].unsqueeze(2).broadcast_to([128, ng, 16])
                ins = v.tensor_tensor(
                    out_ap.rearrange("p (g k) -> p g k", k=16),
                    b.qq[:, 0:ncols].rearrange("p (g k) -> p g k", k=16),
                    scb, ALU.mult)
                self.done = b.eV.tick(ins)

            def s(self, s):
                pass

            def g(self, g):
                pass

            def v2(self, v):
                pass

        return QJ()



def build(dbg=(), upto="p1c"):
    nc = bass.Bass("TRN2", target_bir_lowering=False)
    ctx = ExitStack()
    with ctx:
        b = B(nc, ctx, dbg=dbg)
        phase_setup(b)
        order = ["p0w", "p0a", "p0c1", "p0c2", "p0b", "p1a", "p1b", "p1c"]
        fns = {"p0w": phase_p0w, "p0a": phase_p0a, "p0c1": phase_p0c1,
               "p0c2": phase_p0c2, "p0b": phase_p0b, "p1a": phase_p1a,
               "p1b": phase_p1b, "p1c": phase_p1c}
        with ExitStack() as qstk:
            QW = I_FFN
            qsb = lambda n, s, d=F32: qstk.enter_context(
                nc.sbuf_tensor(n, s, d))
            b.qsc = qsb("qsc", [128, QW // 16])
            b.qrs = qsb("qrs", [128, QW // 16])
            b.qu = qsb("qu", [128, QW])
            b.qr05 = qsb("qr05", [128, QW])
            b.qq = qsb("qq", [128, QW])
            b.qr2 = qsb("qr2", [128, QW])
            b.qm1 = qsb("qm1", [128, QW], U8)
            b.qm3 = qsb("qm3", [128, QW], U8)
            b.qu2 = qsb("qu2", [128, QW])
            for ph in ["p0w", "p0a", "p0c1"]:
                fns[ph](b)
                b.barrier()
                if ph == upto:
                    break
        if order.index(upto) > order.index("p0c1"):
            for ph in ["p0c2", "p0b", "p1a", "p1b", "p1c"]:
                fns[ph](b)
                b.barrier()
                if ph == upto:
                    break
        b.flush_dbg_and_finish()
    return nc


def phase_setup(b):
    nc = b.nc
    with ExitStack() as stk:
        iotaF = stk.enter_context(nc.sbuf_tensor("su_iotaF", [128, 896], F32))
        brows = {}
        for nm, n in (("L0_attn_bq", 8), ("L0_attn_bk", 8), ("L0_attn_bv", 8),
                      ("L0_attn_bo", 8), ("L1_attn_b_attn", 24),
                      ("L1_attn_b_proj", 8)):
            brows[nm] = stk.enter_context(
                nc.sbuf_tensor("su_br_" + nm, [n, 128], F32))
        with nc.Block() as blk:
            def g_fn(g):
                for tile, pat, cm in ((iotaF, [[1, 896]], 0),
                                      (b.iotaP, [[1, 1]], 1),
                                      (b.iota8, [[1, 8]], 0),
                                      (b.iota8v, [[1, 1]], 1),
                                      (b.iota768e, [[1, 1]], LCAP)):
                    ins = g.iota(tile[:], pat, channel_multiplier=cm,
                                 allow_small_or_imprecise_dtypes=True)
                b.eG.tick(ins)

            def sy_fn(sy):
                for nm, tile in brows.items():
                    n = tile.shape[0]
                    ap = b.in_ap[nm].rearrange("(m p) -> m p", p=128)
                    b.dS.dtick(sy.dma_start(out=tile[:], in_=ap))

            def v_fn(v):
                v.wait_ge(b.eG.h, b.eG.n)
                for tile, val in ((b.ones1f, 1.0), (b.ones8f, 1.0),
                                  (b.ones1f8, 1.0), (b.onetokb, 1.0),
                                  (b.kvacc, 0.0), (b.ksacc, 0.0),
                                  (b.qrm, 0.0), (b.krm, 0.0)):
                    v.memset(tile[:], val)
                v.tensor_scalar(b.identf[:], iotaF[:, 0:128], b.iotaP[:, 0:1],
                                None, ALU.is_equal)
                v.tensor_scalar(b.identb[:], iotaF[:, 0:128], b.iotaP[:, 0:1],
                                None, ALU.is_equal)
                v.tensor_scalar(b.maskM[:], iotaF[:], b.iotaP[:, 0:1], 384.0,
                                ALU.subtract, ALU.is_ge)
                v.tensor_scalar(b.a9[:], b.iota8[:], -1.0, 9.0, ALU.mult,
                                ALU.add)
                v.tensor_scalar(b.h2[:, 0:1], b.iotaP[:], 64.0, None, ALU.is_lt)
                ins = v.tensor_scalar(b.h2[:, 1:2], b.iotaP[:], 64.0, None,
                                      ALU.is_ge)
                b.eV.tick(ins)

            blk.gpsimd(g_fn)
            blk.sync(sy_fn)
            blk.vector(v_fn)

        with nc.Block() as blk2, ExitStack() as st:
            psb = st.enter_context(nc.psum_tensor("ps_bias", [128, 128], F32))
            items = [(brows["L0_attn_bq"], b.bq_c, 8),
                     (brows["L0_attn_bk"], b.bk_c, 8),
                     (brows["L0_attn_bv"], b.bv_c, 8),
                     (brows["L0_attn_bo"], b.bo_c, 8),
                     (brows["L1_attn_b_attn"], b.battn_c, 24),
                     (brows["L1_attn_b_proj"], b.bproj_c, 8)]
            recs = {}
            for i, (srct, dst, n) in enumerate(items):
                def t_one(t, i=i, srct=srct, n=n):
                    if i == 0:
                        t.wait_ge(b.dS.h, b.dS.n)
                        t.wait_ge(b.eV.h, b.eV.n)
                    else:
                        t.wait_ge(b.eV.h, recs[i - 1])
                    recs[("t", i)] = b.eT.tick(
                        t.transpose(psb[0:128, 0:n], srct[0:n, :],
                                    b.identf[0:n, 0:n]))

                def v_one(v, i=i, dst=dst, n=n):
                    v.wait_ge(b.eT.h, recs[("t", i)])
                    recs[i] = b.eV.tick(
                        v.tensor_copy(dst[:, 0:n], psb[0:128, 0:n]))

                blk2.tensor(t_one)
                blk2.vector(v_one)
    b.barrier()


def phase_p0w(b):
    nc = b.nc
    with ExitStack() as st:
        wch = st.enter_context(nc.sbuf_tensor("p0w_wch", [128, I_FFN], F32))
        wqt = st.enter_context(nc.sbuf_tensor("p0w_wqt", [128, I_FFN], BF16))
        bands = [st.enter_context(nc.sbuf_tensor(f"p0w_bd{k}", [128, I_FFN], BF16))
                 for k in range(8)]
        pst = [st.enter_context(nc.psum_tensor(f"p0w_ps{j}", [128, 512], BF16))
               for j in range(2)]

        def do_weight(wname, outname, nrowch, rowlen, nkc):
            w_in = b.in_ap[wname]
            w_out = b.scr[outname]
            batch = max(1, I_FFN // rowlen)
            kgroups = [(k, min(k + 8, nkc)) for k in range(0, nkc, 8)]
            for (k0, k1) in kgroups:
                last_copy = {}
                c = 0
                while c < nrowch:
                    nload = min(batch, nrowch - c)
                    ncols = nload * rowlen
                    with nc.Block() as blk:
                        rec = {}
                        qj = b.quant_job(wch[:, 0:ncols], wqt[:, 0:ncols], ncols)

                        def sy_all(sy, c=c, nload=nload):
                            b.bwait(sy)
                            sy.wait_ge(b.eV.h, b.eV.n)
                            sy.wait_ge(b.eT.h, b.eT.n)
                            for j in range(nload):
                                b.dS.dtick(sy.dma_start(
                                    out=wch[:, j * rowlen:(j + 1) * rowlen],
                                    in_=w_in[128 * (c + j):128 * (c + j + 1), :]))
                            rec["ld"] = b.dS.n

                        blk.sync(sy_all)
                        blk.vector(lambda v: (v.wait_ge(b.dS.h, rec["ld"]),
                                              qj.v1(v)))
                        blk.scalar(qj.s)
                        blk.gpsimd(qj.g)
                        blk.vector(qj.v2)

                        items = [(j, kk) for j in range(nload)
                                 for kk in range(k0, k1)]
                        for gi in range(0, len(items), 4):
                            grp = items[gi:gi + 4]
                            par = (gi // 4) % 2

                            def t_grp(t, grp=grp, par=par):
                                t.wait_ge(b.eV.h, qj.done)
                                prev = last_copy.get(par)
                                if prev:
                                    t.wait_ge(b.eV.h, prev)
                                for qi, (j, kk) in enumerate(grp):
                                    col0 = j * rowlen + 128 * kk
                                    ins = t.transpose(
                                        pst[par][:, 128 * qi:128 * (qi + 1)],
                                        wqt[:, col0:col0 + 128], b.identb[:])
                                rec[("t", gi)] = b.eT.tick(ins)

                            def v_grp(v, grp=grp, par=par, gi=gi, c=c):
                                v.wait_ge(b.eT.h, rec[("t", gi)])
                                for qi, (j, kk) in enumerate(grp):
                                    cc = c + j
                                    ins = v.tensor_copy(
                                        bands[kk - k0][:, 128 * cc:128 * (cc + 1)],
                                        pst[par][:, 128 * qi:128 * (qi + 1)])
                                last_copy[par] = b.eV.tick(ins)

                            blk.tensor(t_grp)
                            blk.vector(v_grp)
                    c += nload
                with nc.Block() as blk:
                    def sy_st(sy, k0=k0, k1=k1):
                        sy.wait_ge(b.eV.h, b.eV.n)
                        for kk in range(k0, k1):
                            b.dS.dtick(sy.dma_start(
                                out=w_out[128 * kk:128 * (kk + 1), 0:128 * nrowch],
                                in_=bands[kk - k0][:, 0:128 * nrowch]))
                    blk.sync(sy_st)
                b.barrier()

        do_weight("L0_ffn_w_gate", "s_wgq", NI, D, 8)
        do_weight("L0_ffn_w_up", "s_wuq", NI, D, 8)
        do_weight("L0_ffn_w_down", "s_wdq", 8, I_FFN, NI)
    b.barrier()


def phase_p0a(b):
    nc = b.nc
    with ExitStack() as st:
        sbn = lambda n, s, d=F32: st.enter_context(nc.sbuf_tensor(n, s, d))
        xt = sbn("p0a_xt", [128, 4096])
        xnt = sbn("p0a_xnt", [128, 4096])
        sqs = xnt
        xnb = sbn("p0a_xnb", [128, 4096], BF16)
        xqt = sbn("p0a_xqt", [128, 4096], BF16)
        xnfm = [sbn(f"p0a_xnfm{k}", [128, 512], BF16) for k in range(8)]
        xqfm = [sbn(f"p0a_xqfm{k}", [128, 512], BF16) for k in range(8)]
        Wq_sb = [sbn(f"p0a_wq{k}", [128, 1024], BF16) for k in range(8)]
        Wk_sb = [sbn(f"p0a_wk{k}", [128, 1024], BF16) for k in range(8)]
        Wv_sb = [sbn(f"p0a_wv{k}", [128, 1024], BF16) for k in range(8)]
        qsto = [sbn(f"p0a_qs{k}", [128, 512]) for k in range(8)]
        lsum = sbn("p0a_lsum", [128, 4])
        lssq = sbn("p0a_lssq", [128, 4])
        ltmp = sbn("p0a_ltmp", [128, 4])
        lvar = sbn("p0a_lvar", [128, 4])
        lsrt = sbn("p0a_lsrt", [128, 4])
        lrstd = sbn("p0a_lrstd", [128, 4])
        lmean = sbn("p0a_lmean", [128, 4])
        lred = sbn("p0a_lred", [128, 1])
        fa = sbn("p0a_fa", [128, 512])
        fe = sbn("p0a_fe", [128, 512])
        fr = sbn("p0a_fr", [128, 512])
        kf = sbn("p0a_kf", [128, 512])
        vp = sbn("p0a_vp", [128, 512])
        zz = sbn("p0a_zz", [128, 512])
        pst = [st.enter_context(nc.psum_tensor(f"p0a_pst{j}", [128, 512], BF16))
               for j in range(2)]
        pq_ = [st.enter_context(nc.psum_tensor(f"p0a_pq{j}", [128, 512], F32))
               for j in range(2)]
        pk_ = [st.enter_context(nc.psum_tensor(f"p0a_pk{j}", [128, 512], F32))
               for j in range(2)]
        pv_ = [st.enter_context(nc.psum_tensor(f"p0a_pv{j}", [128, 512], F32))
               for j in range(2)]

        with nc.Block() as blk:
            def g_w(g):
                b.bwait(g)
                for wnm, tiles in (("L0_attn_Wq", Wq_sb), ("L0_attn_Wk", Wk_sb),
                                   ("L0_attn_Wv", Wv_sb)):
                    for k in range(8):
                        b.dG.dtick(g.dma_start(
                            out=tiles[k][:],
                            in_=b.in_ap[wnm][128 * k:128 * (k + 1), :]))
            blk.gpsimd(g_w)
        w_ld = b.dG.n

        for t_i in range(NT):
            bb = t_i // 4
            rec = {}
            with nc.Block() as blk:
                qjs = [b.quant_job(xnt[:, 2048 * hh:2048 * (hh + 1)],
                                   xqt[:, 2048 * hh:2048 * (hh + 1)], 2048)
                       for hh in range(2)]

                def sy_ld(sy, t_i=t_i):
                    b.bwait(sy)
                    sy.wait_ge(b.eV.h, b.eV.n)
                    sy.wait_ge(b.eS.h, b.eS.n)
                    sy.wait_ge(b.eT.h, b.eT.n)
                    sy.wait_ge(b.eG.h, b.eG.n)
                    for ch in range(4):
                        r0 = TT * t_i + 128 * ch
                        b.dS.dtick(sy.dma_start(
                            out=xt[:, 1024 * ch:1024 * (ch + 1)],
                            in_=b.in_ap["x"][r0:r0 + 128, :]))
                    rec["ld"] = b.dS.n

                def s_sq(s):
                    s.wait_ge(b.dS.h, rec["ld"])
                    rec["sq"] = b.eS.tick(s.activation(sqs[:], xt[:], AF.Square))

                def v_ln(v):
                    v.wait_ge(b.dS.h, rec["ld"])
                    v.wait_ge(b.eS.h, rec["sq"])
                    x3 = xt[:].rearrange("p (c d) -> p c d", d=1024)
                    s3 = sqs[:].rearrange("p (c d) -> p c d", d=1024)
                    v.tensor_reduce(lsum[:].unsqueeze(2), x3, AX.X, ALU.add)
                    b.gap(v, b.eV, v.tensor_reduce(lssq[:].unsqueeze(2), s3,
                                                   AX.X, ALU.add))
                    b.gap(v, b.eV, v.scalar_tensor_tensor(
                        ltmp[:], lsum[:], 1.0 / 1024.0, lsum[:], ALU.mult,
                        ALU.mult))
                    b.gap(v, b.eV, v.tensor_tensor(lvar[:], lssq[:], ltmp[:],
                                                   ALU.subtract))
                    rec["ve"] = b.eV.tick(v.tensor_scalar(
                        lvar[:], lvar[:], 1.0 / 1024.0, 1e-5, ALU.mult, ALU.add))
                    v.wait_ge(b.eV.h, rec["ve"])

                def s_sqrt(s):
                    s.wait_ge(b.eV.h, rec["ve"])
                    s.activation(lsrt[:], lvar[:], AF.Ln)
                    rec["srt"] = b.eS.tick(s.activation(lrstd[:], lsrt[:],
                                                        AF.Exp, scale=-0.5))

                def v_xn(v):
                    v.wait_ge(b.eS.h, rec["srt"])
                    b.gap(v, b.eV, v.tensor_scalar(lmean[:], lsum[:],
                                                   1.0 / 1024.0, None, ALU.mult))
                    for ch in range(4):
                        v.tensor_scalar(xnt[:, 1024 * ch:1024 * (ch + 1)],
                                        xt[:, 1024 * ch:1024 * (ch + 1)],
                                        lmean[:, ch:ch + 1], lrstd[:, ch:ch + 1],
                                        ALU.subtract, ALU.mult)
                    rec["xn"] = b.eV.tick(v.tensor_copy(xnb[:], xnt[:]))

                blk.sync(sy_ld)
                blk.scalar(s_sq)
                blk.vector(v_ln)
                blk.scalar(s_sqrt)
                blk.vector(v_xn)
                for qj in qjs:
                    blk.vector(qj.v1)
                    blk.scalar(qj.s)
                    blk.gpsimd(qj.g)
                    blk.vector(qj.v2)
                qj_done = b.eV.n

                last_copy = {}
                for which, src, dsts, wkey in (("xn", xnb, xnfm, "xn"),
                                               ("xq", xqt, xqfm, None)):
                    for kc in range(8):
                        par = kc % 2

                        def t_grp(t, src=src, kc=kc, par=par, wkey=wkey):
                            if wkey:
                                t.wait_ge(b.eV.h, rec[wkey])
                            else:
                                t.wait_ge(b.eV.h, qj_done)
                            prev = last_copy.get(par)
                            if prev:
                                t.wait_ge(b.eV.h, prev)
                            for ch in range(4):
                                ins = t.transpose(
                                    pst[par][:, 128 * ch:128 * (ch + 1)],
                                    src[:, 1024 * ch + 128 * kc:
                                        1024 * ch + 128 * (kc + 1)],
                                    b.identb[:])
                            rec[("t", src.name, kc)] = b.eT.tick(ins)

                        def v_grp(v, src=src, dsts=dsts, kc=kc, par=par):
                            v.wait_ge(b.eT.h, rec[("t", src.name, kc)])
                            ins = v.tensor_copy(dsts[kc][:], pst[par][:])
                            last_copy[par] = b.eV.tick(ins)
                            rec[("fm", dsts[kc].name)] = last_copy[par]

                        blk.tensor(t_grp)
                        blk.vector(v_grp)
                xn_fm_done = b.eV.n

                col = None
                last_evac = {}
                for mc in range(8):
                    par_ = mc % 2
                    pq, pk, pv = pq_[par_], pk_[par_], pv_[par_]

                    def t_qkv(t, mc=mc, pq=pq, pk=pk, pv=pv):
                        if mc == 0:
                            t.wait_ge(b.dG.h, w_ld)
                            for k in range(8):
                                t.wait_ge(b.eV.h, rec[("fm", xnfm[k].name)])
                        for pp, W in ((pq, Wq_sb), (pk, Wk_sb), (pv, Wv_sb)):
                            prev = last_evac.get(pp.name)
                            if prev:
                                t.wait_ge(b.eV.h, prev)
                            for k in range(8):
                                ins = t.matmul(pp[:],
                                               W[k][:, 128 * mc:128 * (mc + 1)],
                                               xnfm[k][:], start=(k == 0),
                                               stop=(k == 7))
                            rec[("mm", pp.name, mc)] = b.eT.tick(ins)

                    def v_fq(v, mc=mc, pq=pq):
                        v.wait_ge(b.eT.h, rec[("mm", pq.name, mc)])
                        rec[("faq", mc)] = b.eV.tick(v.tensor_scalar(
                            fa[:], pq[:], b.bq_c[:, mc:mc + 1], 0.0, ALU.add,
                            ALU.min))

                    def s_fq(s, mc=mc):
                        s.wait_ge(b.eV.h, rec[("faq", mc)])
                        rec[("feq", mc)] = b.eS.tick(
                            s.activation(fe[:], fa[:], AF.Exp))

                    def v_q2(v, mc=mc, pq=pq, pk=pk):
                        v.tensor_scalar(fr[:], pq[:], b.bq_c[:, mc:mc + 1], 0.0,
                                        ALU.add, ALU.max)
                        v.wait_ge(b.eS.h, rec[("feq", mc)])
                        ins = v.tensor_tensor(qsto[mc][:], fr[:], fe[:], ALU.add)
                        rec[("qf", mc)] = b.eV.tick(ins)
                        last_evac[pq.name] = rec[("qf", mc)]
                        v.wait_ge(b.eT.h, rec[("mm", pk.name, mc)])
                        rec[("fak", mc)] = b.eV.tick(v.tensor_scalar(
                            fa[:], pk[:], b.bk_c[:, mc:mc + 1], 0.0, ALU.add,
                            ALU.min))

                    def s_fk(s, mc=mc):
                        s.wait_ge(b.eV.h, rec[("fak", mc)])
                        rec[("fek", mc)] = b.eS.tick(
                            s.activation(fe[:], fa[:], AF.Exp))

                    def v_k2(v, mc=mc, bb=bb, pk=pk, pv=pv):
                        v.tensor_scalar(fr[:], pk[:], b.bk_c[:, mc:mc + 1], 0.0,
                                        ALU.add, ALU.max)
                        v.wait_ge(b.eS.h, rec[("fek", mc)])
                        v.tensor_tensor(kf[:], fr[:], fe[:], ALU.add)
                        v.wait_ge(b.eT.h, rec[("mm", pv.name, mc)])
                        v.tensor_scalar(vp[:], pv[:], b.bv_c[:, mc:mc + 1], None,
                                        ALU.add)
                        ins = v.tensor_tensor(zz[:], kf[:], vp[:], ALU.mult)
                        n = b.eV.tick(ins)
                        last_evac[pk.name] = n
                        last_evac[pv.name] = n
                        cc = mc * 2 + bb
                        b.gap(v, b.eV, v.tensor_reduce(lred[:], zz[:], AX.X,
                                                       ALU.add))
                        b.gap(v, b.eV, v.tensor_tensor(
                            b.kvacc[:, cc:cc + 1], b.kvacc[:, cc:cc + 1],
                            lred[:], ALU.add))
                        b.gap(v, b.eV, v.tensor_reduce(lred[:], kf[:], AX.X,
                                                       ALU.add))
                        b.gap(v, b.eV, v.tensor_tensor(
                            b.ksacc[:, cc:cc + 1], b.ksacc[:, cc:cc + 1],
                            lred[:], ALU.add))

                    blk.tensor(t_qkv)
                    blk.vector(v_fq)
                    blk.scalar(s_fq)
                    blk.vector(v_q2)
                    blk.scalar(s_fk)
                    blk.vector(v_k2)

                def sy_st(sy, t_i=t_i):
                    for mc in range(8):
                        sy.wait_ge(b.eV.h, rec[("qf", mc)])
                        b.dS.dtick(sy.dma_start(
                            out=b.scr["s_q0"][128 * mc:128 * (mc + 1),
                                              TT * t_i:TT * (t_i + 1)],
                            in_=qsto[mc][:]))
                    for kc in range(8):
                        sy.wait_ge(b.eV.h, rec[("fm", xqfm[kc].name)])
                        b.dS.dtick(sy.dma_start(
                            out=b.scr["s_xq"][128 * kc:128 * (kc + 1),
                                              TT * t_i:TT * (t_i + 1)],
                            in_=xqfm[kc][:]))

                blk.sync(sy_st)
        b.dbg_store("kvacc", b.kvacc[:], [])
        b.dbg_store("ksacc", b.ksacc[:], [])
    b.barrier()


def phase_p0c1(b):
    nc = b.nc
    NTS = [(0, 512), (512, 1024), (1024, 1536), (1536, 2048), (2048, 2560),
           (2560, 2816)]
    with ExitStack() as st:
        sbn = lambda n, s, d=F32: st.enter_context(nc.sbuf_tensor(n, s, d))
        wgq_sb = [sbn(f"c1_wg{k}", [128, I_FFN], BF16) for k in range(8)]
        wuq_sb = [sbn(f"c1_wu{k}", [128, I_FFN], BF16) for k in range(8)]
        xqt = [sbn(f"c1_xq{k}", [128, 512], BF16) for k in range(8)]
        sg = sbn("c1_sg", [128, 512])
        ht = sbn("c1_h", [128, I_FFN])
        hqt = sbn("c1_hq", [128, I_FFN], BF16)
        bands = [sbn(f"c1_bd{k}", [128, 512], BF16) for k in range(NI)]
        pg = [st.enter_context(nc.psum_tensor(f"c1_pg{j}", [128, 512], F32))
              for j in range(2)]
        pu = [st.enter_context(nc.psum_tensor(f"c1_pu{j}", [128, 512], F32))
              for j in range(2)]
        pst = [st.enter_context(nc.psum_tensor(f"c1_ps{j}", [128, 512], BF16))
               for j in range(2)]

        with nc.Block() as blk:
            def sy_w(sy):
                b.bwait(sy)
                for tiles, nm in ((wgq_sb, "s_wgq"), (wuq_sb, "s_wuq")):
                    for k in range(8):
                        b.dS.dtick(sy.dma_start(
                            out=tiles[k][:],
                            in_=b.scr[nm][128 * k:128 * (k + 1), :]))
            blk.sync(sy_w)
        w_ld = b.dS.n

        for t_i in range(NT):
            rec = {}
            with nc.Block() as blk:
                def sy_ld(sy, t_i=t_i):
                    sy.wait_ge(b.eT.h, b.eT.n)
                    for k in range(8):
                        b.dS.dtick(sy.dma_start(
                            out=xqt[k][:],
                            in_=b.scr["s_xq"][128 * k:128 * (k + 1),
                                              TT * t_i:TT * (t_i + 1)]))
                    rec["ld"] = b.dS.n
                blk.sync(sy_ld)

                for tc in range(4):
                    for ni, (n0, n1) in enumerate(NTS):
                        par = ni % 2

                        def t_mm(t, tc=tc, ni=ni, n0=n0, n1=n1, t_i=t_i,
                                 par=par):
                            if t_i == 0 and tc == 0 and ni == 0:
                                t.wait_ge(b.dS.h, w_ld)
                            t.wait_ge(b.dS.h, rec["ld"])
                            prev = rec.get(("hm_p", par))
                            if prev:
                                t.wait_ge(b.eV.h, prev)
                            for k in range(8):
                                t.matmul(pg[par][:, 0:n1 - n0],
                                         xqt[k][:, 128 * tc:128 * (tc + 1)],
                                         wgq_sb[k][:, n0:n1], start=(k == 0),
                                         stop=False)
                            for k in range(8):
                                ins = t.matmul(pu[par][:, 0:n1 - n0],
                                               xqt[k][:, 128 * tc:128 * (tc + 1)],
                                               wuq_sb[k][:, n0:n1],
                                               start=(k == 0), stop=(k == 7))
                            rec[("mm", tc, ni)] = b.eT.tick(ins)

                        def s_si(s, tc=tc, ni=ni, n0=n0, n1=n1, par=par):
                            s.wait_ge(b.eT.h, rec[("mm", tc, ni)])
                            prev = rec.get(("hm_p", 1 - par))
                            if prev:
                                s.wait_ge(b.eV.h, prev)
                            rec[("si", tc, ni)] = b.eS.tick(s.activation(
                                sg[:, 0:n1 - n0], pg[par][:, 0:n1 - n0],
                                AF.Silu))

                        def v_h(v, tc=tc, ni=ni, n0=n0, n1=n1, par=par):
                            v.wait_ge(b.eS.h, rec[("si", tc, ni)])
                            ins = v.tensor_tensor(ht[:, n0:n1],
                                                  sg[:, 0:n1 - n0],
                                                  pu[par][:, 0:n1 - n0],
                                                  ALU.mult)
                            rec[("hm", tc, ni)] = b.eV.tick(ins)
                            rec[("hm_p", par)] = rec[("hm", tc, ni)]

                        blk.tensor(t_mm)
                        blk.scalar(s_si)
                        blk.vector(v_h)

                    qj = b.quant_job(ht[:], hqt[:], I_FFN)
                    blk.vector(qj.v1)
                    blk.scalar(qj.s)
                    blk.gpsimd(qj.g)
                    prev_tr = rec.get(("tr", tc - 1, 20))
                    if prev_tr:
                        blk.vector(lambda v, p=prev_tr: v.wait_ge(b.eT.h, p))
                    blk.vector(qj.v2)

                    for gi in range(0, NI, 4):
                        kks = list(range(gi, min(gi + 4, NI)))
                        par = (gi // 4) % 2

                        def t_tr(t, kks=kks, par=par, tc=tc, qj=qj):
                            t.wait_ge(b.eV.h, qj.done)
                            prev = rec.get(("bc", par))
                            if prev:
                                t.wait_ge(b.eV.h, prev)
                            for qi, kk in enumerate(kks):
                                ins = t.transpose(
                                    pst[par][:, 128 * qi:128 * (qi + 1)],
                                    hqt[:, 128 * kk:128 * (kk + 1)], b.identb[:])
                            rec[("tr", tc, gi)] = b.eT.tick(ins)

                        def v_bc(v, kks=kks, par=par, tc=tc, gi=gi):
                            v.wait_ge(b.eT.h, rec[("tr", tc, gi)])
                            for qi, kk in enumerate(kks):
                                ins = v.tensor_copy(
                                    bands[kk][:, 128 * tc:128 * (tc + 1)],
                                    pst[par][:, 128 * qi:128 * (qi + 1)])
                            rec[("bc", par)] = b.eV.tick(ins)
                            rec[("bcd", tc, gi)] = rec[("bc", par)]

                        blk.tensor(t_tr)
                        blk.vector(v_bc)

                def sy_st(sy, t_i=t_i):
                    sy.wait_ge(b.eV.h, b.eV.n)
                    for kk in range(NI):
                        b.dS.dtick(sy.dma_start(
                            out=b.scr["s_hq"][128 * kk:128 * (kk + 1),
                                              TT * t_i:TT * (t_i + 1)],
                            in_=bands[kk][:]))
                blk.sync(sy_st)
    b.barrier()


def phase_p0c2(b):
    nc = b.nc
    with ExitStack() as st:
        sbn = lambda n, s, d=F32: st.enter_context(nc.sbuf_tensor(n, s, d))
        wdq_sb = [sbn(f"c2_wd{k}", [128, D], BF16) for k in range(NI)]
        hqt = [sbn(f"c2_hq{k}", [128, 512], BF16) for k in range(NI)]
        fft = [sbn(f"c2_ff{j}", [128, 512]) for j in range(2)]
        pd = [st.enter_context(nc.psum_tensor(f"c2_pd{j}", [128, 512], F32))
              for j in range(2)]

        with nc.Block() as blk:
            def sy_w(sy):
                b.bwait(sy)
                for k in range(NI):
                    b.dS.dtick(sy.dma_start(
                        out=wdq_sb[k][:],
                        in_=b.scr["s_wdq"][128 * k:128 * (k + 1), :]))
            blk.sync(sy_w)
        w_ld = b.dS.n

        for t_i in range(NT):
            rec = {}
            with nc.Block() as blk:
                def sy_ld(sy, t_i=t_i):
                    sy.wait_ge(b.eT.h, b.eT.n)
                    for k in range(NI):
                        b.dS.dtick(sy.dma_start(
                            out=hqt[k][:],
                            in_=b.scr["s_hq"][128 * k:128 * (k + 1),
                                              TT * t_i:TT * (t_i + 1)]))
                    rec["ld"] = b.dS.n
                blk.sync(sy_ld)

                for mc in range(8):
                    par = mc % 2

                    def t_mm(t, mc=mc, par=par, t_i=t_i):
                        if t_i == 0 and mc == 0:
                            t.wait_ge(b.dS.h, w_ld)
                        t.wait_ge(b.dS.h, rec["ld"])
                        prev = rec.get(("ev", par))
                        if prev:
                            t.wait_ge(b.eV.h, prev)
                        for k in range(NI):
                            ins = t.matmul(pd[par][:],
                                           wdq_sb[k][:, 128 * mc:128 * (mc + 1)],
                                           hqt[k][:], start=(k == 0),
                                           stop=(k == NI - 1))
                        rec[("mm", mc)] = b.eT.tick(ins)

                    def v_ev(v, mc=mc, par=par):
                        v.wait_ge(b.eT.h, rec[("mm", mc)])
                        prevst = rec.get(("st", par))
                        if prevst:
                            v.wait_ge(b.dS.h, prevst)
                        ins = v.tensor_copy(fft[par][:], pd[par][:])
                        rec[("ev", par)] = b.eV.tick(ins)

                    def sy_one(sy, mc=mc, par=par, t_i=t_i):
                        sy.wait_ge(b.eV.h, rec[("ev", par)])
                        b.dS.dtick(sy.dma_start(
                            out=b.scr["s_ffn0"][128 * mc:128 * (mc + 1),
                                                TT * t_i:TT * (t_i + 1)],
                            in_=fft[par][:]))
                        rec[("st", par)] = b.dS.n

                    blk.tensor(t_mm)
                    blk.vector(v_ev)
                    blk.sync(sy_one)
    b.barrier()


def phase_p0b(b):
    nc = b.nc
    with ExitStack() as st:
        sbn = lambda n, s, d=F32: st.enter_context(nc.sbuf_tensor(n, s, d))
        Wo_sb = [sbn(f"b_wo{k}", [128, D], BF16) for k in range(8)]
        qt = [sbn(f"b_q{k}", [128, 512]) for k in range(8)]
        fft = [sbn(f"b_ff{k}", [128, 512]) for k in range(8)]
        xt = sbn("b_xt", [128, 4096])
        x1t = sbn("b_x1", [128, 4096])
        vl = [sbn(f"b_vl{k}", [128, 512], BF16) for k in range(8)]
        t1s = [sbn(f"b_t1{k}", [128, 512]) for k in range(8)]
        z8 = [sbn(f"b_z8{k}", [128, 512]) for k in range(8)]
        uu = sbn("b_uu", [128, 512])
        go = sbn("b_go", [128, 512])
        af = sbn("b_af", [128, 512])
        po = [st.enter_context(nc.psum_tensor(f"b_po{j}", [128, 512], F32))
              for j in range(2)]
        ptf = [st.enter_context(nc.psum_tensor(f"b_ptf{j}", [128, 512], F32))
               for j in range(2)]

        with nc.Block() as blk:
            def g_w(g):
                b.bwait(g)
                for k in range(8):
                    b.dG.dtick(g.dma_start(
                        out=Wo_sb[k][:],
                        in_=b.in_ap["L0_attn_Wo"][128 * k:128 * (k + 1), :]))
            blk.gpsimd(g_w)
        w_ld = b.dG.n

        for t_i in range(NT):
            bb = t_i // 4
            rec = {}
            with nc.Block() as blk:
                def sy_ld(sy, t_i=t_i):
                    sy.wait_ge(b.eV.h, b.eV.n)
                    sy.wait_ge(b.eS.h, b.eS.n)
                    sy.wait_ge(b.eT.h, b.eT.n)
                    for k in range(8):
                        b.dS.dtick(sy.dma_start(
                            out=qt[k][:],
                            in_=b.scr["s_q0"][128 * k:128 * (k + 1),
                                              TT * t_i:TT * (t_i + 1)]))
                    for k in range(8):
                        b.dS.dtick(sy.dma_start(
                            out=fft[k][:],
                            in_=b.scr["s_ffn0"][128 * k:128 * (k + 1),
                                                TT * t_i:TT * (t_i + 1)]))
                    for ch in range(4):
                        r0 = TT * t_i + 128 * ch
                        b.dS.dtick(sy.dma_start(
                            out=xt[:, 1024 * ch:1024 * (ch + 1)],
                            in_=b.in_ap["x"][r0:r0 + 128, :]))
                    rec["ld"] = b.dS.n
                blk.sync(sy_ld)

                def v_t1all(v, bb=bb):
                    v.wait_ge(b.dS.h, rec["ld"])
                    v.wait_ge(b.dG.h, w_ld)
                    for kc in range(8):
                        cc = kc * 2 + bb
                        ins = v.tensor_scalar(
                            t1s[kc][:], qt[kc][:], b.ksacc[:, cc:cc + 1], 1e-6,
                            ALU.mult, ALU.add)
                    rec["t1"] = b.eV.tick(ins)

                def s_zall(s):
                    s.wait_ge(b.eV.h, rec["t1"])
                    for kc in range(8):
                        s.activation(z8[kc][:], t1s[kc][:], AF.Ln)
                    for kc in range(8):
                        ins = s.activation(t1s[kc][:], z8[kc][:], AF.Exp,
                                           scale=-1.0)
                    rec["z"] = b.eS.tick(ins)

                def v_vlall(v, bb=bb):
                    v.wait_ge(b.eS.h, rec["z"])
                    for kc in range(8):
                        cc = kc * 2 + bb
                        v.tensor_scalar(uu[:], qt[kc][:], b.kvacc[:, cc:cc + 1],
                                        None, ALU.mult)
                        rec[("vl", kc)] = b.eV.tick(
                            v.tensor_tensor(vl[kc][:], uu[:], t1s[kc][:],
                                            ALU.mult))

                blk.vector(v_t1all)
                blk.scalar(s_zall)
                blk.vector(v_vlall)

                for mc in range(8):
                    par = mc % 2

                    def t_mm(t, mc=mc, par=par):
                        for k in range(8):
                            t.wait_ge(b.eV.h, rec[("vl", k)])
                        prev = rec.get(("go", par))
                        if prev:
                            t.wait_ge(b.eS.h, prev)
                        for k in range(8):
                            ins = t.matmul(po[par][:],
                                           Wo_sb[k][:, 128 * mc:128 * (mc + 1)],
                                           vl[k][:], start=(k == 0),
                                           stop=(k == 7))
                        rec[("mm", mc)] = b.eT.tick(ins)

                    def s_go(s, mc=mc, par=par):
                        s.wait_ge(b.eT.h, rec[("mm", mc)])
                        prev = rec.get(("af", 1 - par))
                        if prev:
                            s.wait_ge(b.eV.h, prev)
                        rec[("go", par)] = b.eS.tick(s.activation(
                            go[:], po[par][:], AF.Gelu_apprx_tanh,
                            bias=b.bo_c[:, mc:mc + 1]))

                    def v_af(v, mc=mc, par=par):
                        v.wait_ge(b.eS.h, rec[("go", par)])
                        prev = rec.get(("tr", mc - 1))
                        if prev:
                            v.wait_ge(b.eT.h, prev)
                        rec[("af", par)] = b.eV.tick(
                            v.tensor_tensor(af[:], go[:], fft[mc][:], ALU.add))

                    def t_tr(t, mc=mc, par=par):
                        t.wait_ge(b.eV.h, rec[("af", par)])
                        prev = rec.get(("x1", par))
                        if prev:
                            t.wait_ge(b.eV.h, prev)
                        for tc in range(4):
                            ins = t.transpose(
                                ptf[par][:, 128 * tc:128 * (tc + 1)],
                                af[:, 128 * tc:128 * (tc + 1)], b.identf[:])
                        rec[("tr", mc)] = b.eT.tick(ins)

                    def v_x1(v, mc=mc, par=par):
                        v.wait_ge(b.eT.h, rec[("tr", mc)])
                        for tc in range(4):
                            col = 1024 * tc + 128 * mc
                            ins = v.tensor_tensor(
                                x1t[:, col:col + 128],
                                ptf[par][:, 128 * tc:128 * (tc + 1)],
                                xt[:, col:col + 128], ALU.add)
                        rec[("x1", par)] = b.eV.tick(ins)

                    blk.tensor(t_mm)
                    blk.scalar(s_go)
                    blk.vector(v_af)
                    blk.tensor(t_tr)
                    blk.vector(v_x1)

                def sy_st(sy, t_i=t_i):
                    sy.wait_ge(b.eV.h, b.eV.n)
                    for ch in range(4):
                        r0 = TT * t_i + 128 * ch
                        b.dS.dtick(sy.dma_start(
                            out=b.scr["s_x1"][r0:r0 + 128, :],
                            in_=x1t[:, 1024 * ch:1024 * (ch + 1)]))
                blk.sync(sy_st)
    b.barrier()


def phase_p1a(b):
    nc = b.nc
    with ExitStack() as st:
        sbn = lambda n, s, d=F32: st.enter_context(nc.sbuf_tensor(n, s, d))
        x1t = sbn("a1_x1", [128, 4096])
        xnt = sbn("a1_xn", [128, 4096])
        sqs = xnt
        xnb = sbn("a1_xnb", [128, 4096], BF16)
        xnfm = [sbn(f"a1_fm{k}", [128, 512], BF16) for k in range(8)]
        Wa_sb = [sbn(f"a1_wa{k}", [128, 3 * D], BF16) for k in range(8)]
        battn_row = sbn("a1_brow", [1, 3 * D], BF16)
        qke = [sbn(f"a1_qke{j}", [128, 512], BF16) for j in range(4)]
        vtile = sbn("a1_vt", [128, NH * 65], BF16)
        lsum = sbn("a1_lsum", [128, 4])
        lssq = sbn("a1_lssq", [128, 4])
        ltmp = sbn("a1_ltmp", [128, 4])
        lvar = sbn("a1_lvar", [128, 4])
        lsrt = sbn("a1_lsrt", [128, 4])
        lrstd = sbn("a1_lrstd", [128, 4])
        lmean = sbn("a1_lmean", [128, 4])
        lred = sbn("a1_lred", [128, 1])
        pst = [st.enter_context(nc.psum_tensor(f"a1_ps{j}", [128, 512], BF16))
               for j in range(2)]
        pqk = [st.enter_context(nc.psum_tensor(f"a1_pqk{j}", [128, 512], F32))
               for j in range(2)]
        pv = [st.enter_context(nc.psum_tensor(f"a1_pv{j}", [128, 512], F32))
              for j in range(2)]

        with nc.Block() as blk:
            def g_w(g):
                b.bwait(g)
                for k in range(8):
                    b.dG.dtick(g.dma_start(
                        out=Wa_sb[k][:],
                        in_=b.in_ap["L1_attn_W_attn"][128 * k:128 * (k + 1), :]))
                b.dG.dtick(g.dma_start(
                    out=battn_row[:], in_=b.in_ap["L1_attn_b_attn"].unsqueeze(0)))
            blk.gpsimd(g_w)
        w_ld = b.dG.n

        for t_i in range(NT):
            bb = t_i // 4
            rec = {}
            with nc.Block() as blk:
                def sy_ld(sy, t_i=t_i):
                    sy.wait_ge(b.eV.h, b.eV.n)
                    sy.wait_ge(b.eS.h, b.eS.n)
                    sy.wait_ge(b.eT.h, b.eT.n)
                    sy.wait_ge(b.eG.h, b.eG.n)
                    for ch in range(4):
                        r0 = TT * t_i + 128 * ch
                        b.dS.dtick(sy.dma_start(
                            out=x1t[:, 1024 * ch:1024 * (ch + 1)],
                            in_=b.scr["s_x1"][r0:r0 + 128, :]))
                    rec["ld"] = b.dS.n

                def s_sq(s):
                    s.wait_ge(b.dS.h, rec["ld"])
                    rec["sq"] = b.eS.tick(s.activation(sqs[:], x1t[:], AF.Square))

                def v_ln(v):
                    v.wait_ge(b.dS.h, rec["ld"])
                    v.wait_ge(b.eS.h, rec["sq"])
                    x3 = x1t[:].rearrange("p (c d) -> p c d", d=1024)
                    s3 = sqs[:].rearrange("p (c d) -> p c d", d=1024)
                    v.tensor_reduce(lsum[:].unsqueeze(2), x3, AX.X, ALU.add)
                    b.gap(v, b.eV, v.tensor_reduce(lssq[:].unsqueeze(2), s3,
                                                   AX.X, ALU.add))
                    b.gap(v, b.eV, v.scalar_tensor_tensor(
                        ltmp[:], lsum[:], 1.0 / 1024.0, lsum[:], ALU.mult,
                        ALU.mult))
                    b.gap(v, b.eV, v.tensor_tensor(lvar[:], lssq[:], ltmp[:],
                                                   ALU.subtract))
                    rec["ve"] = b.eV.tick(v.tensor_scalar(
                        lvar[:], lvar[:], 1.0 / 1024.0, 1e-5, ALU.mult, ALU.add))
                    v.wait_ge(b.eV.h, rec["ve"])

                def s_sqrt(s):
                    s.wait_ge(b.eV.h, rec["ve"])
                    s.activation(lsrt[:], lvar[:], AF.Ln)
                    rec["srt"] = b.eS.tick(s.activation(lrstd[:], lsrt[:],
                                                        AF.Exp, scale=-0.5))

                def v_xn(v):
                    v.wait_ge(b.eS.h, rec["srt"])
                    b.gap(v, b.eV, v.tensor_scalar(lmean[:], lsum[:],
                                                   1.0 / 1024.0, None, ALU.mult))
                    for ch in range(4):
                        v.tensor_scalar(xnt[:, 1024 * ch:1024 * (ch + 1)],
                                        x1t[:, 1024 * ch:1024 * (ch + 1)],
                                        lmean[:, ch:ch + 1], lrstd[:, ch:ch + 1],
                                        ALU.subtract, ALU.mult)
                    rec["xn"] = b.eV.tick(v.tensor_copy(xnb[:], xnt[:]))

                def sy_xn(sy, t_i=t_i):
                    sy.wait_ge(b.eV.h, rec["xn"])
                    for ch in range(4):
                        r0 = TT * t_i + 128 * ch
                        b.dS.dtick(sy.dma_start(
                            out=b.scr["s_xn1"][r0:r0 + 128, :],
                            in_=xnt[:, 1024 * ch:1024 * (ch + 1)]))
                        b.dS.dtick(sy.dma_start(
                            out=b.scr["s_xn1b"][r0:r0 + 128, :],
                            in_=xnb[:, 1024 * ch:1024 * (ch + 1)]))

                blk.sync(sy_ld)
                blk.scalar(s_sq)
                blk.vector(v_ln)
                blk.scalar(s_sqrt)
                blk.vector(v_xn)
                blk.sync(sy_xn)

                last_copy = {}
                for kc in range(8):
                    par = kc % 2

                    def t_grp(t, kc=kc, par=par):
                        t.wait_ge(b.eV.h, rec["xn"])
                        prev = last_copy.get(par)
                        if prev:
                            t.wait_ge(b.eV.h, prev)
                        for ch in range(4):
                            ins = t.transpose(
                                pst[par][:, 128 * ch:128 * (ch + 1)],
                                xnb[:, 1024 * ch + 128 * kc:
                                    1024 * ch + 128 * (kc + 1)],
                                b.identb[:])
                        rec[("t", kc)] = b.eT.tick(ins)

                    def v_grp(v, kc=kc, par=par):
                        v.wait_ge(b.eT.h, rec[("t", kc)])
                        ins = v.tensor_copy(xnfm[kc][:], pst[par][:])
                        last_copy[par] = b.eV.tick(ins)
                        rec[("fm", kc)] = last_copy[par]

                    blk.tensor(t_grp)
                    blk.vector(v_grp)

                for mc in range(16):
                    par = mc % 2
                    parq = mc % 4
                    acc = b.qrm if mc < 8 else b.krm
                    cc = (mc % 8) * 2 + bb

                    def t_mm(t, mc=mc, par=par):
                        if mc == 0:
                            t.wait_ge(b.dG.h, w_ld)
                            for k in range(8):
                                t.wait_ge(b.eV.h, rec[("fm", k)])
                        prev2 = rec.get(("rm", par))
                        if prev2:
                            t.wait_ge(b.eV.h, prev2)
                        for k in range(8):
                            ins = t.matmul(pqk[par][:],
                                           Wa_sb[k][:, 128 * mc:128 * (mc + 1)],
                                           xnfm[k][:], start=(k == 0),
                                           stop=(k == 7))
                        rec[("mm", mc)] = b.eT.tick(ins)

                    def v_ev(v, mc=mc, par=par, parq=parq, acc=acc, cc=cc):
                        v.wait_ge(b.eT.h, rec[("mm", mc)])
                        prev = rec.get(("qkd", parq))
                        if prev:
                            v.wait_ge(b.dS.h, prev)
                        ins = v.tensor_scalar(qke[parq][:], pqk[par][:],
                                              b.battn_c[:, mc:mc + 1], None,
                                              ALU.add)
                        rec[("rm", par)] = b.eV.tick(ins)

                    def sy_qk(sy, mc=mc, parq=parq, t_i=t_i):
                        sy.wait_ge(b.eV.h, rec[("rm", mc % 2)])
                        b.dS.dtick(sy.dma_start(
                            out=b.scr["s_qk"][128 * mc:128 * (mc + 1),
                                              TT * t_i:TT * (t_i + 1)],
                            in_=qke[parq][:]))
                        rec[("qkd", parq)] = b.dS.n

                    blk.tensor(t_mm)
                    blk.vector(v_ev)
                    blk.sync(sy_qk)

                def v_m1(v):
                    prev = rec.get("vt_store")
                    ins = v.memset(vtile[:], 1.0)
                    rec["vms"] = b.eV.tick(ins)

                blk.vector(v_m1)
                for tc in range(4):
                    for half in range(2):
                        par = half

                        def t_v(t, tc=tc, half=half, par=par):
                            prev = rec.get(("vev", par))
                            if prev:
                                t.wait_ge(b.eV.h, prev)
                            for k in range(8):
                                t.matmul(pv[par][:],
                                         xnfm[k][:, 128 * tc:128 * (tc + 1)],
                                         Wa_sb[k][:, 2 * D + 512 * half:
                                                   2 * D + 512 * (half + 1)],
                                         start=(k == 0), stop=False)
                            ins = t.matmul(
                                pv[par][:], b.onetokb[:],
                                battn_row[:, 2 * D + 512 * half:
                                          2 * D + 512 * (half + 1)],
                                start=False, stop=True)
                            rec[("vmm", tc, half)] = b.eT.tick(ins)

                        def v_vev(v, tc=tc, half=half, par=par):
                            v.wait_ge(b.eV.h, rec["vms"])
                            v.wait_ge(b.eT.h, rec[("vmm", tc, half)])
                            src3 = pv[par][:].rearrange("p (h j) -> p h j", j=64)
                            dst3 = vtile[:, 65 * 8 * half:65 * 8 * (half + 1)] \
                                .rearrange("p (h j) -> p h j", j=65)[:, :, 0:64]
                            ins = v.tensor_copy(dst3, src3)
                            rec[("vev", par)] = b.eV.tick(ins)

                        blk.tensor(t_v)
                        blk.vector(v_vev)

                    def sy_v(sy, tc=tc, t_i=t_i):
                        sy.wait_ge(b.eV.h, rec[("vev", 0)])
                        sy.wait_ge(b.eV.h, rec[("vev", 1)])
                        r0 = TT * t_i + 128 * tc
                        b.dS.dtick(sy.dma_start(
                            out=b.scr["s_v"][r0:r0 + 128, :], in_=vtile[:]))
                        rec["vt_store"] = b.dS.n

                    blk.sync(sy_v)

                    if tc < 3:
                        def v_m2(v, tc=tc):
                            v.wait_ge(b.dS.h, rec["vt_store"])
                            rec["vms"] = b.eV.tick(v.memset(vtile[:], 1.0))
                        blk.vector(v_m2)
    b.dbg_store("qrm", b.qrm[:], [])
    b.dbg_store("krm", b.krm[:], [])
    b.barrier()


def phase_p1b(b):
    nc = b.nc
    with ExitStack() as st:
        sbn = lambda n, s, d=F32: st.enter_context(nc.sbuf_tensor(n, s, d))
        qh = [sbn(f"f_qh{j}", [64, S], BF16) for j in range(2)]
        kh = [sbn(f"f_kh{j}", [64, S], BF16) for j in range(2)]
        vb = [sbn(f"f_vb{k}", [128, NH * 65], BF16) for k in range(16)]
        Osb = [sbn(f"f_o{k}", [128, S], BF16) for k in range(8)]
        Pt = [sbn(f"f_p{j}", [128, 512], BF16) for j in range(4)]
        Lrow = sbn("f_l", [65, S])
        RL = sbn("f_rl", [65, S])
        ones65 = sbn("f_o65", [65, 128])
        at = [sbn(f"f_at{j}", [128, 512]) for j in range(2)]
        a1t = sbn("f_a1", [128, 4096])
        Wp_sb = [sbn(f"f_wp{k}", [128, D], BF16) for k in range(8)]
        pS = [st.enter_context(nc.psum_tensor(f"f_pS{j}", [128, 512], F32))
              for j in range(4)]
        pO = st.enter_context(nc.psum_tensor("f_pO", [128, 512], F32))
        pB = [st.enter_context(nc.psum_tensor(f"f_pB{j}", [128, 512], F32))
              for j in range(2)]

        rec = {}
        with nc.Block() as blk:
            def g_w(g):
                b.bwait(g)
                for k in range(8):
                    b.dG.dtick(g.dma_start(
                        out=Wp_sb[k][:],
                        in_=b.in_ap["L1_attn_W_proj"][128 * k:128 * (k + 1), :]))
            blk.gpsimd(g_w)

            def v_pre(v):
                b.bwait(v)
                ins = v.memset(ones65[:], 1.0)
                b.eV.tick(ins)
            blk.vector(v_pre)

        for bb in range(2):
            with nc.Block() as blk:
                def sy_vb(sy, bb=bb):
                    sy.wait_ge(b.eT.h, b.eT.n)
                    for k in range(16):
                        r0 = S * bb + 128 * k
                        b.dS.dtick(sy.dma_start(
                            out=vb[k][:], in_=b.scr["s_v"][r0:r0 + 128, :]))
                    rec["vb"] = b.dS.n
                blk.sync(sy_vb)

            for h in range(NH):
                mcol = (h % 2) * 16 + (h // 2) * 2 + bb
                ch = h // 2
                rb = 64 * (h % 2)
                hp = h % 2
                eT0, eS0, eV0 = b.eT.n, b.eS.n, b.eV.n
                tS, tE, tM, tPV, tOev, tOsc, tBC = {}, {}, {}, {}, {}, {}, {}
                nT, nS, nV = eT0, eS0, eV0
                for qt in range(4):
                    nkc = 4 * qt + 4
                    for kc in range(min(4, nkc)):
                        nT += 1
                        tS[(qt, kc)] = nT
                    for kc in range(nkc):
                        if kc == nkc - 1:
                            nT += 1
                            tPV[qt] = nT
                        if kc + 4 < nkc:
                            nT += 1
                            tS[(qt, kc + 4)] = nT
                    for kc in range(nkc):
                        nS += 1
                        tE[(qt, kc)] = nS
                        if kc >= 4 * qt:
                            nV += 1
                            tM[(qt, kc)] = nV
                    nV += 1
                    tOev[qt] = nV
                nS += 1
                tRL = nS
                for qt in range(4):
                    nT += 1
                    tBC[qt] = nT
                    nV += 1
                    tOsc[qt] = nV
                prev_hS = rec.get(("hS", h - 2))
                prev_rl = rec.get("rl")
                prev_oev = rec.get("oev")
                prev_osc_live = [rec.get(("osc", 0)), rec.get(("osc", 1))]
                prev_expdone = dict(rec.get("expdone_map", {}))

                with nc.Block() as blk:
                    def sy_qk(sy, bb=bb, h=h, hp=hp, prev_hS=prev_hS):
                        if prev_hS:
                            sy.wait_ge(b.eT.h, prev_hS)
                        elif h < 2 and bb == 0:
                            sy.wait_ge(b.eT.h, b.eT.n)
                        elif h < 2:
                            sy.wait_ge(b.eT.h, rec[("bbproj", bb - 1)])
                        b.dS.dtick(sy.dma_start(
                            out=qh[hp][:],
                            in_=b.scr["s_qk"][64 * h:64 * (h + 1),
                                              S * bb:S * (bb + 1)]))
                        b.dS.dtick(sy.dma_start(
                            out=kh[hp][:],
                            in_=b.scr["s_qk"][D + 64 * h:D + 64 * (h + 1),
                                              S * bb:S * (bb + 1)]))
                        rec["qk"] = b.dS.n

                    def v_all(v, h=h, hp=hp, mcol=mcol, ch=ch, rb=rb,
                              prev_hS=prev_hS, prev_rl=prev_rl):
                        if prev_hS:
                            v.wait_ge(b.eT.h, prev_hS)
                        for qt in range(4):
                            for kc in range(4 * qt, 4 * qt + 4):
                                v.wait_ge(b.eS.h, tE[(qt, kc)])
                                off = 128 * kc - 512 * qt
                                par = kc % 4
                                ins = v.tensor_tensor(
                                    Pt[par][:], Pt[par][:],
                                    b.maskM[:, 384 - off:384 - off + 512],
                                    ALU.mult)
                                assert b.eV.tick(ins) == tM[(qt, kc)]
                            v.wait_ge(b.eT.h, tPV[qt])
                            if qt == 0 and prev_rl:
                                v.wait_ge(b.eS.h, prev_rl)
                            v.tensor_copy(
                                Osb[ch][rb:rb + 64, 512 * qt:512 * (qt + 1)],
                                pO[0:64, :])
                            ins = v.tensor_copy(
                                Lrow[64:65, 512 * qt:512 * (qt + 1)],
                                pO[64:65, :])
                            assert b.eV.tick(ins) == tOev[qt]
                        for qt in range(4):
                            v.wait_ge(b.eT.h, tBC[qt])
                            sl = Osb[ch][rb:rb + 64, 512 * qt:512 * (qt + 1)]
                            par2 = qt % 2
                            ins = v.tensor_tensor(sl, sl,
                                                  pB[par2][rb:rb + 64, :],
                                                  ALU.mult)
                            assert b.eV.tick(ins) == tOsc[qt]

                    def s_all(s, h=h):
                        for qt in range(4):
                            for kc in range(4 * qt + 4):
                                par = kc % 4
                                s.wait_ge(b.eT.h, tS[(qt, kc)])
                                ins = s.activation(Pt[par][:], pS[par][:],
                                                   AF.Exp, scale=0.125)
                                assert b.eS.tick(ins) == tE[(qt, kc)]
                        s.wait_ge(b.eV.h, tOev[3])
                        s.activation(RL[64:65, :], Lrow[64:65, :], AF.Ln)
                        ins = s.activation(Lrow[64:65, :], RL[64:65, :],
                                           AF.Exp, scale=-1.0)
                        assert b.eS.tick(ins) == tRL
                        rec["rl"] = tRL

                    def t_all(t, h=h, hp=hp, prev_oev=prev_oev,
                              prev_osc_live=prev_osc_live,
                              prev_expdone=prev_expdone):
                        t.wait_ge(b.dS.h, rec["qk"])
                        t.wait_ge(b.dS.h, rec["vb"])
                        last_exp_par = dict(prev_expdone)
                        last_oev = prev_oev

                        def emit_S(qt, kc):
                            par = kc % 4
                            prev = last_exp_par.get(par)
                            if prev:
                                t.wait_ge(b.eS.h, prev)
                            ins = t.matmul(
                                pS[par][:], kh[hp][:, 128 * kc:128 * (kc + 1)],
                                qh[hp][:, 512 * qt:512 * (qt + 1)],
                                start=True, stop=True)
                            assert b.eT.tick(ins) == tS[(qt, kc)]
                            last_exp_par[par] = tE[(qt, kc)]

                        for qt in range(4):
                            nkc = 4 * qt + 4
                            for kc in range(min(4, nkc)):
                                emit_S(qt, kc)
                            for kc in range(nkc):
                                par = kc % 4
                                diag = kc >= 4 * qt
                                if diag:
                                    t.wait_ge(b.eV.h, tM[(qt, kc)])
                                else:
                                    t.wait_ge(b.eS.h, tE[(qt, kc)])
                                if kc == 0 and last_oev:
                                    t.wait_ge(b.eV.h, last_oev)
                                ins = t.matmul(pO[0:65, :],
                                               vb[kc][:, 65 * h:65 * (h + 1)],
                                               Pt[par][:], start=(kc == 0),
                                               stop=(kc == nkc - 1))
                                if kc == nkc - 1:
                                    assert b.eT.tick(ins) == tPV[qt]
                                    last_oev = tOev[qt]
                                if kc + 4 < nkc:
                                    emit_S(qt, kc + 4)
                        for qt in range(4):
                            par2 = qt % 2
                            t.wait_ge(b.eS.h, tRL)
                            prevo = (prev_osc_live[par2] if qt < 2
                                     else tOsc[qt - 2])
                            if prevo:
                                t.wait_ge(b.eV.h, prevo)
                            ins = t.matmul(pB[par2][:], ones65[64:65, :],
                                           Lrow[64:65, 512 * qt:512 * (qt + 1)],
                                           start=True, stop=True)
                            assert b.eT.tick(ins) == tBC[qt]
                        rec["expdone_map"] = last_exp_par

                    blk.sync(sy_qk)
                    blk.scalar(s_all)
                    blk.vector(v_all)
                    blk.tensor(t_all)
                rec["oev"] = tOev[3]
                rec[("osc", 0)] = tOsc[2]
                rec[("osc", 1)] = tOsc[3]
                rec[("hS", h)] = b.eT.n

            with nc.Block() as blk:
                o_done = b.eV.n
                for qt in range(4):
                    for mc in range(8):
                        par = mc % 2

                        def t_pp(t, qt=qt, mc=mc, par=par, o_done=o_done):
                            if qt == 0 and mc == 0:
                                t.wait_ge(b.eV.h, o_done)
                                t.wait_ge(b.dG.h, b.dG.n)
                            prev = rec.get(("pgo", par))
                            if prev:
                                t.wait_ge(b.eV.h, prev)
                            for k in range(8):
                                ins = t.matmul(
                                    pS[par][:],
                                    Wp_sb[k][:, 128 * mc:128 * (mc + 1)],
                                    Osb[k][:, 512 * qt:512 * (qt + 1)],
                                    start=(k == 0), stop=(k == 7))
                            rec[("pp", qt, mc)] = b.eT.tick(ins)

                        def s_ev(s, qt=qt, mc=mc, par=par):
                            s.wait_ge(b.eT.h, rec[("pp", qt, mc)])
                            prev = rec.get(("a1", par))
                            if prev:
                                s.wait_ge(b.eT.h, prev)
                            ins = s.tensor_scalar(at[par][:], pS[par][:],
                                                  b.bproj_c[:, mc:mc + 1],
                                                  None, ALU.add)
                            rec[("pgo", par)] = b.eV.tick(ins)

                        def t_tr(t, qt=qt, mc=mc, par=par):
                            t.wait_ge(b.eV.h, rec[("pgo", par)])
                            prev = rec.get(("a1c", par))
                            if prev:
                                t.wait_ge(b.eV.h, prev)
                            for tc in range(4):
                                ins = t.transpose(
                                    pB[par][:, 128 * tc:128 * (tc + 1)],
                                    at[par][:, 128 * tc:128 * (tc + 1)],
                                    b.identf[:])
                            rec[("a1", par)] = b.eT.tick(ins)

                        def v_a1(v, qt=qt, mc=mc, par=par):
                            v.wait_ge(b.eT.h, rec[("a1", par)])
                            prevst = rec.get("a1st")
                            if prevst and mc < 2:
                                v.wait_ge(b.dS.h, prevst)
                            for tc in range(4):
                                col = 1024 * tc + 128 * mc
                                ins = v.tensor_copy(
                                    a1t[:, col:col + 128],
                                    pB[par][:, 128 * tc:128 * (tc + 1)])
                            rec[("a1c", par)] = b.eV.tick(ins)

                        blk.tensor(t_pp)
                        blk.vector(s_ev)
                        blk.tensor(t_tr)
                        blk.vector(v_a1)

                    def sy_a1(sy, qt=qt, bb=bb):
                        sy.wait_ge(b.eV.h, b.eV.n)
                        for tc in range(4):
                            r0 = S * bb + 512 * qt + 128 * tc
                            b.dS.dtick(sy.dma_start(
                                out=b.scr["s_attn1"][r0:r0 + 128, :],
                                in_=a1t[:, 1024 * tc:1024 * (tc + 1)]))
                        rec["a1st"] = b.dS.n

                    blk.sync(sy_a1)
            rec[("bbproj", bb)] = b.eT.n
    b.barrier()


def phase_p1c(b):
    nc = b.nc
    rec = {}
    with ExitStack() as st:
        sbn = lambda n, s, d=F32: st.enter_context(nc.sbuf_tensor(n, s, d))
        xnbt = sbn("c_xnb", [128, 4096], BF16)
        xnfm = [sbn(f"c_fm{k}", [128, 512], BF16) for k in range(8)]
        Wr_sb = [sbn(f"c_wr{k}", [128, 8], BF16) for k in range(8)]
        r_m = sbn("c_m", [128, 1])
        r_mneg = sbn("c_mneg", [128, 1])
        r_e8 = sbn("c_e8", [128, 8])
        r_s8 = sbn("c_s8", [128, 1])
        r_eq = sbn("c_eq", [128, 8])
        r_mx = sbn("c_mx", [128, 1])
        X1 = sbn("c_X1", [8, T])
        X2 = sbn("c_X2", [8, T])
        X3 = sbn("c_X3", [8, T])
        R1 = sbn("c_R1", [1, T])
        R2 = sbn("c_R2", [1, T])
        R3 = sbn("c_R3", [1, T])
        R4 = sbn("c_R4", [1, T])
        idxT = sbn("c_idxT", [32, 128])
        backT = sbn("c_backT", [32, 128])
        pr = st.enter_context(nc.psum_tensor("c_pr", [128, 512], F32))
        pT = st.enter_context(nc.psum_tensor("c_pT", [128, 512], F32))
        pst = [st.enter_context(nc.psum_tensor(f"c_ps{j}", [128, 512], BF16))
               for j in range(2)]

        with nc.Block() as blk:
            def g_pre(g):
                b.bwait(g)
                b.eG.tick(g.iota(R1[:], [[1, T]], channel_multiplier=0,
                                 allow_small_or_imprecise_dtypes=True))
                for k in range(8):
                    b.dG.dtick(g.dma_start(
                        out=Wr_sb[k][:],
                        in_=b.in_ap["L1_ffn_Wr"][128 * k:128 * (k + 1), :]))

            def v_pre(v):
                v.wait_ge(b.eG.h, b.eG.n)
                v.memset(X3[:], 0.0)
                rec["tbig"] = b.eV.tick(v.tensor_scalar(
                    R1[:], R1[:], 1.0, BIG, ALU.mult, ALU.add))

            blk.gpsimd(g_pre)
            blk.vector(v_pre)
        w_ld = b.dG.n

        for t_i in range(NT):
            with nc.Block() as blk:
                def sy_ld(sy, t_i=t_i):
                    sy.wait_ge(b.eT.h, b.eT.n)
                    for ch in range(4):
                        r0 = TT * t_i + 128 * ch
                        b.dS.dtick(sy.dma_start(
                            out=xnbt[:, 1024 * ch:1024 * (ch + 1)],
                            in_=b.scr["s_xn1b"][r0:r0 + 128, :]))
                    rec["ld"] = b.dS.n
                blk.sync(sy_ld)

                last_copy = {}
                for kc in range(8):
                    par = kc % 2

                    def t_grp(t, kc=kc, par=par):
                        t.wait_ge(b.dS.h, rec["ld"])
                        prev = last_copy.get(par)
                        if prev:
                            t.wait_ge(b.eV.h, prev)
                        for ch in range(4):
                            ins = t.transpose(
                                pst[par][:, 128 * ch:128 * (ch + 1)],
                                xnbt[:, 1024 * ch + 128 * kc:
                                     1024 * ch + 128 * (kc + 1)],
                                b.identb[:])
                        rec[("t", kc)] = b.eT.tick(ins)

                    def v_grp(v, kc=kc, par=par):
                        v.wait_ge(b.eT.h, rec[("t", kc)])
                        ins = v.tensor_copy(xnfm[kc][:], pst[par][:])
                        last_copy[par] = b.eV.tick(ins)
                        rec[("fm", kc)] = last_copy[par]

                    blk.tensor(t_grp)
                    blk.vector(v_grp)

                for tc in range(4):
                    col = t_i * 4 + tc

                    def t_r(t, tc=tc, t_i=t_i):
                        if tc == 0 and t_i == 0:
                            t.wait_ge(b.dG.h, w_ld)
                        for k in range(8):
                            t.wait_ge(b.eV.h, rec[("fm", k)])
                        prev = rec.get("pr_free")
                        if prev:
                            t.wait_ge(b.eV.h, prev)
                        for k in range(8):
                            ins = t.matmul(pr[:, 0:8],
                                           xnfm[k][:, 128 * tc:128 * (tc + 1)],
                                           Wr_sb[k][:], start=(k == 0),
                                           stop=(k == 7))
                        rec[("rmm", col)] = b.eT.tick(ins)

                    def v_r1(v, col=col):
                        v.wait_ge(b.eT.h, rec[("rmm", col)])
                        b.gap(v, b.eV, v.tensor_reduce(r_m[:], pr[:, 0:8],
                                                       AX.X, ALU.max))
                        rec[("rm", col)] = b.eV.tick(v.tensor_scalar(
                            r_mneg[:], r_m[:], -1.0, None, ALU.mult))
                        v.wait_ge(b.eV.h, rec[("rm", col)])

                    def s_r(s, col=col):
                        s.wait_ge(b.eV.h, rec[("rm", col)])
                        rec[("re", col)] = b.eS.tick(s.activation(
                            r_e8[:], pr[:, 0:8], AF.Exp,
                            bias=r_mneg[:, 0:1]))

                    def v_r2(v, col=col):
                        v.wait_ge(b.eS.h, rec[("re", col)])
                        b.gap(v, b.eV, v.tensor_reduce(r_s8[:], r_e8[:], AX.X,
                                                       ALU.add))
                        b.gap(v, b.eV, v.reciprocal(
                            b.pmax_tm[:, col:col + 1], r_s8[:]))
                        b.gap(v, b.eV, v.tensor_tensor(
                            r_eq[:], pr[:, 0:8],
                            r_m[:, 0:1].broadcast_to([128, 8]), ALU.is_equal))
                        rec["pr_free"] = b.eV.n
                        b.gap(v, b.eV, v.tensor_tensor(r_eq[:], r_eq[:],
                                                       b.a9[:], ALU.mult))
                        b.gap(v, b.eV, v.tensor_reduce(r_mx[:], r_eq[:], AX.X,
                                                       ALU.max))
                        b.gap(v, b.eV, v.tensor_scalar(
                            b.idx_tm[:, col:col + 1], r_mx[:], -1.0, 9.0,
                            ALU.mult, ALU.add))

                    blk.tensor(t_r)
                    blk.vector(v_r1)
                    blk.scalar(s_r)
                    blk.vector(v_r2)

        with nc.Block() as blk:
            def t_tp(t):
                t.wait_ge(b.eV.h, b.eV.n)
                rec["tp"] = b.eT.tick(t.transpose(pT[0:32, 0:128], b.idx_tm[:],
                                                  b.identf[:]))

            def v_tp(v):
                v.wait_ge(b.eT.h, rec["tp"])
                rec["idxT"] = b.eV.tick(v.tensor_copy(idxT[:], pT[0:32, 0:128]))

            def sy_rt(sy):
                sy.wait_ge(b.eV.h, rec["idxT"])
                b.dS.dtick(sy.dma_start(
                    out=b.scr["s_rt"].rearrange("(a c) -> a c", c=128),
                    in_=idxT[:]))
                rec["rt_w"] = b.dS.n

            def sy_rt2(sy):
                sy.wait_ge(b.dS.h, rec["rt_w"])
                b.dS.dtick(sy.dma_start(out=R2[:],
                                        in_=b.scr["s_rt"].unsqueeze(0)))
                rec["rt_r"] = b.dS.n

            blk.tensor(t_tp)
            blk.vector(v_tp)
            blk.sync(sy_rt)
            blk.sync(sy_rt2)

            for j in range(8):
                def t_bc(t, j=j):
                    t.wait_ge(b.dS.h, rec["rt_r"])
                    prev = rec.get(("bcc", (j - 1) % 2))
                    if prev:
                        t.wait_ge(b.eV.h, prev)
                    rec[("bc", j)] = b.eT.tick(t.matmul(
                        pT[0:8, 0:512], b.ones1f8[:],
                        R2[:, 512 * j:512 * (j + 1)], start=True, stop=True))

                def v_bc(v, j=j):
                    v.wait_ge(b.eT.h, rec[("bc", j)])
                    ins = v.tensor_copy(X1[:, 512 * j:512 * (j + 1)],
                                        pT[0:8, 0:512])
                    rec[("bcc", j % 2)] = b.eV.tick(ins)

                blk.tensor(t_bc)
                blk.vector(v_bc)

            def v_route(v):
                v.wait_ge(b.eV.h, b.eV.n)
                b.gap(v, b.eV, v.tensor_scalar(X2[:], X1[:],
                                               b.iota8v[:, 0:1], None,
                                               ALU.is_equal))
                b.gap(v, b.eV, v.tensor_tensor_scan(
                    X1[:], X2[:], X3[:], 0.0, ALU.add, ALU.add))
                b.gap(v, b.eV, v.tensor_scalar(X1[:], X1[:], 1.0, None,
                                               ALU.subtract))
                b.gap(v, b.eV, v.tensor_scalar(X3[:], X1[:],
                                               float(LCAP), None, ALU.is_lt))
                b.gap(v, b.eV, v.tensor_tensor(X3[:], X2[:], X3[:], ALU.mult))
                b.gap(v, b.eV, v.tensor_scalar(X2[:], X1[:],
                                               b.iota768e[:, 0:1], None,
                                               ALU.add))
                rec["route"] = b.eV.tick(v.tensor_tensor(X2[:], X2[:], X3[:],
                                                         ALU.mult))
                v.wait_ge(b.eV.h, rec["route"])

            blk.vector(v_route)

            for j in range(8):
                def t_cl(t, j=j):
                    if j == 0:
                        t.wait_ge(b.eV.h, rec["route"])
                    prev = rec.get(("clc", (j - 1) % 2))
                    if prev:
                        t.wait_ge(b.eV.h, prev)
                    t.matmul(pT[0:1, 0:512], b.ones8f[:],
                             X2[:, 512 * j:512 * (j + 1)], start=True,
                             stop=True)
                    rec[("cl", j)] = b.eT.tick(t.matmul(
                        pr[0:1, 0:512], b.ones8f[:],
                        X3[:, 512 * j:512 * (j + 1)], start=True, stop=True))

                def v_cl(v, j=j):
                    v.wait_ge(b.eT.h, rec[("cl", j)])
                    v.tensor_copy(R2[:, 512 * j:512 * (j + 1)],
                                  pT[0:1, 0:512])
                    ins = v.tensor_copy(R3[:, 512 * j:512 * (j + 1)],
                                        pr[0:1, 0:512])
                    rec[("clc", j % 2)] = b.eV.tick(ins)

                blk.tensor(t_cl)
                blk.vector(v_cl)

            def v_dest(v):
                v.wait_ge(b.eV.h, b.eV.n)
                b.gap(v, b.eV, v.tensor_scalar(R4[:], R3[:], -1.0, 1.0,
                                               ALU.mult, ALU.add))
                b.gap(v, b.eV, v.tensor_tensor(R4[:], R4[:], R1[:], ALU.mult))
                rec["drow"] = b.eV.tick(v.tensor_tensor(R1[:], R2[:], R4[:],
                                                        ALU.add))
                v.wait_ge(b.eV.h, rec["drow"])

            blk.vector(v_dest)

            def sy_dr(sy):
                sy.wait_ge(b.eV.h, rec["drow"])
                b.dS.dtick(sy.dma_start(out=b.scr["s_rt2"].unsqueeze(0),
                                        in_=R1[:]))
                b.dS.dtick(sy.dma_start(out=b.scr["s_rt3"].unsqueeze(0),
                                        in_=R3[:]))
                rec["dr_w"] = b.dS.n

            def sy_dr2(sy):
                sy.wait_ge(b.dS.h, rec["dr_w"])
                b.dS.dtick(sy.dma_start(
                    out=backT[:],
                    in_=b.scr["s_rt2"].rearrange("(a c) -> a c", c=128)))
                rec["dr_r"] = b.dS.n

            def t_dt(t):
                t.wait_ge(b.dS.h, rec["dr_r"])
                rec["dt"] = b.eT.tick(t.transpose(pT[0:128, 0:32], backT[:],
                                                  b.identf[0:32, 0:32]))

            def v_dt(v):
                v.wait_ge(b.eT.h, rec["dt"])
                rec["dest"] = b.eV.tick(v.tensor_copy(b.dest_tm[:],
                                                      pT[0:128, 0:32]))

            def sy_kr(sy):
                sy.wait_ge(b.eV.h, rec["dest"])
                b.dS.dtick(sy.dma_start(
                    out=backT[:],
                    in_=b.scr["s_rt3"].rearrange("(a c) -> a c", c=128)))
                rec["kr_r"] = b.dS.n

            def t_kt(t):
                t.wait_ge(b.dS.h, rec["kr_r"])
                rec["kt"] = b.eT.tick(t.transpose(pT[0:128, 0:32], backT[:],
                                                  b.identf[0:32, 0:32]))

            def v_kt(v):
                v.wait_ge(b.eT.h, rec["kt"])
                b.gap(v, b.eV, v.tensor_copy(b.kept_tm[:], pT[0:128, 0:32]))
                b.gap(v, b.eV, v.tensor_scalar(b.factor_tm[:], b.pmax_tm[:],
                                               1.0, None, ALU.subtract))
                b.gap(v, b.eV, v.tensor_tensor(b.factor_tm[:], b.kept_tm[:],
                                               b.factor_tm[:], ALU.mult))
                rec["factor"] = b.eV.tick(v.tensor_scalar(
                    b.factor_tm[:], b.factor_tm[:], 1.0, None, ALU.add))

            blk.sync(sy_dr)
            blk.sync(sy_dr2)
            blk.tensor(t_dt)
            blk.vector(v_dt)
            blk.sync(sy_kr)
            blk.tensor(t_kt)
            blk.vector(v_kt)

        disp_t = [sbn(f"c_dsp{j}", [128, D], BF16) for j in range(2)]
        with nc.Block() as blk:
            for c in range(32):
                par = c % 2

                def sy_dl(sy, c=c, par=par):
                    prev = rec.get(("scat", par))
                    if prev:
                        sy.wait_ge(b.dG.h, prev)
                    b.dS.dtick(sy.dma_start(
                        out=disp_t[par][:],
                        in_=b.scr["s_xn1b"][128 * c:128 * (c + 1), :]))
                    rec[("dl", c)] = b.dS.n

                def g_sc(g, c=c, par=par):
                    g.wait_ge(b.dS.h, rec[("dl", c)])
                    if c == 0:
                        g.wait_ge(b.eV.h, rec["dest"])
                    if "bc_reg" not in rec:
                        rec["bc_reg"] = g.to_reg(NEXP * LCAP - 1)
                    ins = g.indirect_dma_start(
                        out=b.scr["s_D"],
                        out_offset=bass.IndirectOffsetOnAxis(
                            ap=b.dest_tm[:, c:c + 1], axis=0),
                        in_=disp_t[par][:],
                        in_offset=None,
                        bounds_check=rec["bc_reg"],
                        oob_is_err=False)
                    rec[("scat", par)] = b.dG.dtick(ins)

                blk.sync(sy_dl)
                blk.gpsimd(g_sc)
        scat_done = b.dG.n

    with ExitStack() as st:
        sbn = lambda n, s, d=F32: st.enter_context(nc.sbuf_tensor(n, s, d))
        W1_sb = [sbn(f"c_w1_{k}", [128, 2048], BF16) for k in range(8)]
        W2_sb = [sbn(f"c_w2_{k}", [128, D], BF16) for k in range(32)]
        Dtm = [sbn(f"c_dtm{k}", [128, D], BF16) for k in range(6)]
        Dfm = [sbn(f"c_dfm{k}", [128, LCAP], BF16) for k in range(8)]
        Hfm = [sbn(f"c_hf{k}", [128, LCAP], BF16) for k in range(32)]
        eo = [sbn(f"c_eo{j}", [128, D], F32) for j in range(2)]
        pst = [st.enter_context(nc.psum_tensor(f"c_eps{j}", [128, 512], BF16))
               for j in range(2)]
        pm1 = [st.enter_context(nc.psum_tensor(f"c_pm1{j}", [128, 512], F32))
               for j in range(2)]
        pm2 = [st.enter_context(nc.psum_tensor(f"c_pm2{j}", [128, 512], F32))
               for j in range(2)]

        for e in range(NEXP):
            with nc.Block() as blk:
                def g_w1(g, e=e):
                    if e == 0:
                        g.wait_ge(b.dG.h, scat_done)
                        g.wait_ge(b.eV.h, b.eV.n)
                    g.wait_ge(b.eT.h, b.eT.n)
                    for k in range(8):
                        b.dG.dtick(g.dma_start(
                            out=W1_sb[k][:],
                            in_=b.in_ap["L1_ffn_W1"][e, 128 * k:128 * (k + 1),
                                                     0:2048]))
                    for k in range(32):
                        b.dG.dtick(g.dma_start(
                            out=W2_sb[k][:],
                            in_=b.in_ap["L1_ffn_W2"][e, 128 * k:128 * (k + 1), :]))
                    rec["w12"] = b.dG.n

                def sy_d(sy, e=e):
                    sy.wait_ge(b.dG.h, scat_done)
                    sy.wait_ge(b.eT.h, b.eT.n)
                    for k in range(6):
                        r0 = LCAP * e + 128 * k
                        b.dS.dtick(sy.dma_start(out=Dtm[k][:],
                                                in_=b.scr["s_D"][r0:r0 + 128, :]))
                    rec["dld"] = b.dS.n

                blk.gpsimd(g_w1)
                blk.sync(sy_d)

                last_copy = {}
                for k in range(6):
                    for kc in range(8):
                        gi = k * 8 + kc
                        par = gi % 2

                        def t_dt2(t, k=k, kc=kc, par=par, gi=gi):
                            t.wait_ge(b.dS.h, rec["dld"])
                            prev = last_copy.get(par)
                            if prev:
                                t.wait_ge(b.eV.h, prev)
                            rec[("dt2", gi)] = b.eT.tick(t.transpose(
                                pst[par][:, 0:128],
                                Dtm[k][:, 128 * kc:128 * (kc + 1)],
                                b.identb[:]))

                        def v_dt2(v, k=k, kc=kc, par=par, gi=gi):
                            v.wait_ge(b.eT.h, rec[("dt2", gi)])
                            ins = v.tensor_copy(
                                Dfm[kc][:, 128 * k:128 * (k + 1)],
                                pst[par][:, 0:128])
                            last_copy[par] = b.eV.tick(ins)

                        blk.tensor(t_dt2)
                        blk.vector(v_dt2)
                dfm_done = b.eV.n

                for fhalf in range(2):
                    if fhalf == 1:
                        def g_w1b(g, e=e):
                            g.wait_ge(b.eT.h, rec[("m1", 15)])
                            for k in range(8):
                                b.dG.dtick(g.dma_start(
                                    out=W1_sb[k][:],
                                    in_=b.in_ap["L1_ffn_W1"][
                                        e, 128 * k:128 * (k + 1), 2048:4096]))
                            rec["w12b"] = b.dG.n
                        blk.gpsimd(g_w1b)

                    for fl in range(16):
                        fc = fhalf * 16 + fl
                        par = fc % 2

                        def t_m1(t, fc=fc, fl=fl, par=par, fhalf=fhalf,
                                 dfm_done=dfm_done):
                            if fc == 0:
                                t.wait_ge(b.eV.h, dfm_done)
                                t.wait_ge(b.dG.h, rec["w12"])
                            if fhalf == 1 and fl == 0:
                                t.wait_ge(b.dG.h, rec["w12b"])
                            prevs = rec.get(("h_ev", par))
                            if prevs:
                                t.wait_ge(b.eV.h, prevs[1])
                            for k in range(8):
                                t.matmul(pm1[par][:, 0:512],
                                         W1_sb[k][:, 128 * fl:128 * (fl + 1)],
                                         Dfm[k][:, 0:512], start=(k == 0),
                                         stop=(k == 7))
                            for k in range(8):
                                ins = t.matmul(
                                    pm2[par][:, 0:256],
                                    W1_sb[k][:, 128 * fl:128 * (fl + 1)],
                                    Dfm[k][:, 512:LCAP], start=(k == 0),
                                    stop=(k == 7))
                            rec[("m1", fc)] = b.eT.tick(ins)

                        def v_rel(v, fc=fc, par=par):
                            v.wait_ge(b.eT.h, rec[("m1", fc)])
                            v.tensor_scalar(Hfm[fc][:, 0:512],
                                            pm1[par][:, 0:512], 0.0, None,
                                            ALU.max)
                            ins = v.tensor_scalar(Hfm[fc][:, 512:LCAP],
                                                  pm2[par][:, 0:256], 0.0,
                                                  None, ALU.max)
                            rec[("h_ev", par)] = (b.eV.n, b.eV.tick(ins))

                        blk.tensor(t_m1)
                        blk.vector(v_rel)

                for rbp in range(3):
                    for sub in range(2):
                        rb = 2 * rbp + sub
                        for half in range(2):
                            pp = (pm1 if sub == 0 else pm2)[half]

                            def t_m2(t, rb=rb, sub=sub, half=half, pp=pp,
                                     rbp=rbp):
                                if rbp == 0 and sub == 0 and half == 0:
                                    t.wait_ge(b.eV.h, b.eV.n)
                                prev = rec.get(("eoc", sub, half))
                                if prev:
                                    t.wait_ge(b.eV.h, prev)
                                for fc in range(32):
                                    ins = t.matmul(
                                        pp[:, 0:512],
                                        Hfm[fc][:, 128 * rb:128 * (rb + 1)],
                                        W2_sb[fc][:, 512 * half:
                                                  512 * (half + 1)],
                                        start=(fc == 0), stop=(fc == 31))
                                rec[("m2", rb, half)] = b.eT.tick(ins)

                            def v_eo(v, rb=rb, sub=sub, half=half, pp=pp):
                                v.wait_ge(b.eT.h, rec[("m2", rb, half)])
                                prev = rec.get(("eost", sub))
                                if prev:
                                    v.wait_ge(b.dS.h, prev)
                                ins = v.tensor_copy(
                                    eo[sub][:, 512 * half:512 * (half + 1)],
                                    pp[:, 0:512])
                                rec[("eoc", sub, half)] = b.eV.tick(ins)

                            blk.tensor(t_m2)
                            blk.vector(v_eo)

                        def sy_eo(sy, rb=rb, sub=sub, e=e):
                            sy.wait_ge(b.eV.h, rec[("eoc", sub, 0)])
                            sy.wait_ge(b.eV.h, rec[("eoc", sub, 1)])
                            r0 = LCAP * e + 128 * rb
                            b.dS.dtick(sy.dma_start(
                                out=b.scr["s_eout"][r0:r0 + 128, :],
                                in_=eo[sub][:]))
                            rec[("eost", sub)] = b.dS.n

                        blk.sync(sy_eo)
        eout_done_S = b.dS.n

    with ExitStack() as st:
        sbn = lambda n, s, d=F32: st.enter_context(nc.sbuf_tensor(n, s, d))
        combt = [sbn(f"c_cb{j}", [128, D], F32) for j in range(4)]
        x1c = [sbn(f"c_x1c{j}", [128, D], F32) for j in range(4)]
        outt = [sbn(f"c_ot{j}", [128, D], F32) for j in range(4)]
        with nc.Block() as blk:
            for c in range(32):
                par = c % 4

                def sy_cl(sy, c=c, par=par):
                    if c == 0:
                        sy.wait_ge(b.dS.h, eout_done_S)
                        sy.wait_ge(b.eV.h, b.eV.n)
                    prev = rec.get(("outst", par))
                    if prev:
                        sy.wait_ge(b.dS.h, prev)
                    prevg = rec.get(("gat", par))
                    if prevg:
                        sy.wait_ge(b.dG.h, prevg)
                    b.dS.dtick(sy.dma_start(
                        out=combt[par][:],
                        in_=b.scr["s_xn1"][128 * c:128 * (c + 1), :]))
                    rec[("cl", c)] = b.dS.n

                def g_cl(g, c=c, par=par):
                    prev = rec.get(("outv", par))
                    if prev:
                        g.wait_ge(b.eV.h, prev)
                    if c == 0:
                        g.wait_ge(b.dS.h, eout_done_S)
                    n0 = b.dG.dtick(g.dma_start(
                        out=x1c[par][:],
                        in_=b.scr["s_x1"][128 * c:128 * (c + 1), :]))
                    g.wait_ge(b.dG.h, n0)
                    b.dG.dtick(g.dma_start(
                        out=x1c[par][:],
                        in_=b.scr["s_attn1"][128 * c:128 * (c + 1), :],
                        accum_op=ALU.add))
                    g.wait_ge(b.dS.h, rec[("cl", c)])
                    if "bc_reg2" not in rec:
                        rec["bc_reg2"] = g.to_reg(NEXP * LCAP - 1)
                    ins = g.indirect_dma_start(
                        out=combt[par][:],
                        out_offset=None,
                        in_=b.scr["s_eout"],
                        in_offset=bass.IndirectOffsetOnAxis(
                            ap=b.dest_tm[:, c:c + 1], axis=0),
                        bounds_check=rec["bc_reg2"],
                        oob_is_err=False)
                    rec[("gat", par)] = b.dG.dtick(ins)

                def v_out(v, c=c, par=par):
                    v.wait_ge(b.dG.h, rec[("gat", par)])
                    v.tensor_scalar(outt[par][:], combt[par][:],
                                    b.factor_tm[:, c:c + 1], None, ALU.mult)
                    ins = v.tensor_tensor(outt[par][:], outt[par][:],
                                          x1c[par][:], ALU.add)
                    rec[("outv", par)] = b.eV.tick(ins)

                def sy_out(sy, c=c, par=par):
                    sy.wait_ge(b.eV.h, rec[("outv", par)])
                    b.dS.dtick(sy.dma_start(
                        out=b.out_ap[128 * c:128 * (c + 1), :],
                        in_=outt[par][:]))
                    rec[("outst", par)] = b.dS.n

                blk.sync(sy_cl)
                blk.gpsimd(g_cl)
                blk.vector(v_out)
                blk.sync(sy_out)
    b.barrier()


def kernel(**inputs):
    x = np.ascontiguousarray(inputs["L0_attn_x"], dtype=np.float32)
    shards = [np.ascontiguousarray(x[2 * c:2 * c + 2].reshape(T, D))
              for c in range(NCORE)]
    nc = build()
    in_maps = []
    for c in range(NCORE):
        m = {"x": shards[c]}
        for nm in W_NAMES:
            m[nm] = np.ascontiguousarray(inputs[nm], dtype=np.float32)
        in_maps.append(m)
    res = run_bass_kernel_spmd(nc, in_maps, core_ids=list(range(NCORE)))
    out = np.concatenate([r["out"].reshape(2, S, D) for r in res.results], axis=0)
    return out
